# version 14
# baseline (speedup 1.0000x reference)
import math
from collections import defaultdict
from contextlib import ExitStack

import numpy as np

import concourse.bass as bass
import concourse.mybir as mybir
from concourse.bass_utils import run_bass_kernel_spmd

F32 = mybir.dt.float32
BF16 = mybir.dt.bfloat16
ALU = mybir.AluOpType
ACT = mybir.ActivationFunctionType
AX = mybir.AxisListType

DR, DN, DV = 64, 128, 128
DQ = DN + DR
S4 = float(DQ) ** (-0.25)

REAL_CFG = dict(HID=2048, S=4096, QLR=1536, KVLR=512, H=16, FFNH=8192, THETA=10000.0)
DEV_CFG = dict(HID=512, S=2048, QLR=384, KVLR=256, H=4, FFNH=2048, THETA=10000.0)


def hadamard(n):
    h = np.array([[1.0]], np.float32)
    while h.shape[0] < n:
        h = np.block([[h, h], [h, -h]]).astype(np.float32)
    return h


def own_tokens(cfg, r):
    NOWNT = (cfg["S"] // 128) // 2
    own = (np.arange(NOWNT)[:, None] * 2 + r) * 128 + np.arange(128)[None, :]
    return own.reshape(-1)


def host_consts(cfg, r):
    S, FFNH = cfg["S"], cfg["FFNH"]
    CW = FFNH // 128
    own = own_tokens(cfg, r)
    c = {}
    c["h128"] = hadamard(128)
    c["m64"] = np.kron(np.eye(128 // CW, dtype=np.float32), hadamard(CW)).astype(np.float32)
    R = np.zeros((DR, DR), np.float32)
    for i in range(DR // 2):
        R[i, i + DR // 2] = -1.0
        R[i + DR // 2, i] = 1.0
    c["r64t"] = np.ascontiguousarray(R.T)
    c["idenf"] = np.eye(128, dtype=np.float32)
    inv = 1.0 / cfg["THETA"] ** (np.arange(0, DR, 2, dtype=np.float64) / DR)
    fr = np.outer(np.arange(S, dtype=np.float64), inv)
    cos64 = np.concatenate([np.cos(fr), np.cos(fr)], 1).T * np.float64(S4)
    sin64 = np.concatenate([np.sin(fr), np.sin(fr)], 1).T * np.float64(S4)
    cos = cos64.astype(np.float32)
    sin = sin64.astype(np.float32)
    cos_lo = (cos64 - cos).astype(np.float32)
    sin_lo = (sin64 - sin).astype(np.float32)
    c["cosf"], c["sinf"] = cos.copy(), sin.copy()
    c["cosf_lo"], c["sinf_lo"] = cos_lo.copy(), sin_lo.copy()
    c["cosfo"] = np.ascontiguousarray(cos[:, own])
    c["sinfo"] = np.ascontiguousarray(sin[:, own])
    c["cosfo_lo"] = np.ascontiguousarray(cos_lo[:, own])
    c["sinfo_lo"] = np.ascontiguousarray(sin_lo[:, own])
    for kd in range(8):
        W = 512 - 128 * (kd // 2)
        coff = 128 * (kd // 2)
        i = np.arange(128)[:, None]
        jj = np.arange(W)[None, :]
        col = coff + jj
        ct = col // 128
        ci = col % 128
        qt = 2 * ct + r
        m = np.where(qt > kd, 1.0, np.where(qt == kd, (ci >= i).astype(np.float32), 0.0))
        c[f"mask{kd}"] = m.astype(np.float32)
    return c


def w1_perm(FFNH):
    CW = FFNH // 128
    i = np.arange(FFNH)
    return (i % CW) * 128 + i // CW


ENGS = ("q", "g", "t", "v", "s")


class Prog:
    def __init__(self):
        self.ops = {e: [] for e in ENGS}
        self.cnt = defaultdict(int)

    def emit(self, eng, fn=None, waits=(), inc=None, amt=1):
        ws = []
        for w in waits:
            if w is None:
                continue
            s, v = w
            if v > 0:
                ws.append((s, v))
        self.ops[eng].append((fn, ws, inc, amt))
        if inc:
            self.cnt[inc] += amt
            return (inc, self.cnt[inc])
        return None

    def tok(self, sem):
        return (sem, self.cnt[sem])


class DmaQ:
    def __init__(self, prog, eng, base, n=8):
        self.prog, self.eng = prog, eng
        self.names = [f"{base}{i}" for i in range(n)]
        self.rr = 0

    def dma(self, fn, waits=()):
        name = self.names[self.rr]
        self.rr = (self.rr + 1) % len(self.names)
        ws = list(waits)
        prev = self.prog.cnt[name]
        if prev:
            ws.append((name, prev))
        return self.prog.emit(self.eng, fn, ws, inc=name, amt=16)

    def all_done(self):
        return [(n, self.prog.cnt[n]) for n in self.names if self.prog.cnt[n]]


class _KList:

    def __init__(self, toks):
        self.toks = list(toks)

    def pop(self):
        t, self.toks = self.toks, []
        return t


class Res:

    def __init__(self, ap=None):
        self.ap = ap
        self.free = []

    def take(self):
        f, self.free = self.free, []
        return f


def build_nc(cfg):
    HID, S, QLR, KVLR, H, FFNH = (cfg[k] for k in ("HID", "S", "QLR", "KVLR", "H", "FFNH"))
    DC = HID // 128
    QC = QLR // 128
    KC = KVLR // 128
    SEQT = S // 128
    NG = SEQT // 4
    NOWNT = SEQT // 2
    OWN = NOWNT * 128
    NBLK = NOWNT // 4
    NPAIR = H // 2
    FC = FFNH // 128
    CW = FFNH // 128
    GRP = 1024 if OWN % 1024 == 0 else 512
    NFG = OWN // GRP
    GN = GRP // 512
    KVW = KVLR + DR
    KVMC = [128] * KC + [DR]
    KVWID = 2 * (DN + DV)
    PHI = 128 // CW

    nc = bass.Bass("TRN2", target_bir_lowering=False)
    dram = lambda n, s, d, k: nc.dram_tensor(n, s, d, kind=k).ap()
    I, O, N = "ExternalInput", "ExternalOutput", "Internal"

    xf = dram("xf", [HID, S], F32, I)
    xt = dram("xt", [S, HID], F32, I)
    xof = dram("xof", [HID, OWN], F32, I)
    xot = dram("xot", [OWN, HID], F32, I)
    wqaT = dram("wqaT", [HID, QLR], F32, I)
    wkvaT = dram("wkvaT", [HID, KVW], F32, I)
    wqbT = dram("wqbT", [QLR, H * DQ], F32, I)
    wkvbT = dram("wkvbT", [KVLR, H * (DN + DV)], F32, I)
    woT = dram("woT", [HID, HID], F32, I)
    w1pt = dram("w1pt", [HID, FFNH], F32, I)
    w3pt = dram("w3pt", [HID, FFNH], F32, I)
    w2t = dram("w2t", [FFNH, HID], F32, I)
    gq_i = dram("gq", [128, QC], F32, I)
    gkv_i = dram("gkv", [128, KC], F32, I)
    h128_i = dram("h128", [128, 128], F32, I)
    m64_i = dram("m64", [128, 128], F32, I)
    r64t_i = dram("r64t", [DR, DR], F32, I)
    idenf_i = dram("idenf", [128, 128], F32, I)
    cosf_i = dram("cosf", [DR, S], F32, I)
    sinf_i = dram("sinf", [DR, S], F32, I)
    cosfo_i = dram("cosfo", [DR, OWN], F32, I)
    sinfo_i = dram("sinfo", [DR, OWN], F32, I)
    wqaT_lo = dram("wqaT_lo", [HID, QLR], F32, I)
    wkvaT_lo = dram("wkvaT_lo", [HID, KVW], F32, I)
    wqbT_lo = dram("wqbT_lo", [QLR, H * DQ], F32, I)
    wkvbT_lo = dram("wkvbT_lo", [KVLR, H * (DN + DV)], F32, I)
    woT_lo = dram("woT_lo", [HID, HID], F32, I)
    w1pt_lo = dram("w1pt_lo", [HID, FFNH], F32, I)
    w3pt_lo = dram("w3pt_lo", [HID, FFNH], F32, I)
    w2t_lo = dram("w2t_lo", [FFNH, HID], F32, I)
    cosf_lo = dram("cosf_lo", [DR, S], F32, I)
    sinf_lo = dram("sinf_lo", [DR, S], F32, I)
    cosfo_lo = dram("cosfo_lo", [DR, OWN], F32, I)
    sinfo_lo = dram("sinfo_lo", [DR, OWN], F32, I)
    mask_i = [dram(f"mask{kd}", [128, 512 - 128 * (kd // 2)], F32, I) for kd in range(8)]
    out = dram("out", [HID, OWN], F32, O)
    ctx_s = dram("ctx_s", [H * DV, OWN], BF16, N)
    y1_s = dram("y1_s", [HID, OWN], F32, N)
    y1n_s = dram("y1n_s", [HID, OWN], BF16, N)
    w2f_s = dram("w2f_s", [FFNH, HID], BF16, N)

    gst = ExitStack()
    sbg = lambda n, s, d: gst.enter_context(nc.sbuf_tensor(n, s, d))
    pb = [gst.enter_context(nc.psum_tensor(f"pb{i}", [128, 512], F32)) for i in range(8)]
    bank = [Res(pb[i]) for i in range(8)]
    semnames = ([f"qd{i}" for i in range(8)] + [f"gd{i}" for i in range(8)]
                + [f"sd{i}" for i in range(8)] + ["ts", "vs", "ss", "gs"])
    sems = {n: gst.enter_context(nc.semaphore(n)) for n in semnames}

    P = Prog()
    Q = DmaQ(P, "q", "qd")
    G = DmaQ(P, "g", "gd")
    SQ = DmaQ(P, "s", "sd")

    def barrier():
        toks = ([("ts", P.cnt["ts"]), ("vs", P.cnt["vs"]), ("ss", P.cnt["ss"])]
                + Q.all_done() + G.all_done() + SQ.all_done())
        for e in ENGS:
            P.emit(e, None, waits=toks)

    ones_c = sbg("ones_c", [128, 1], BF16)
    onesr_f = sbg("onesr_f", [1, 128], F32)
    h128_b = sbg("h128_b", [128, 128], BF16)
    m64_b = sbg("m64_b", [128, 128], BF16)
    r64t_b = sbg("r64t_b", [DR, DR], BF16)
    idenf = sbg("idenf_s", [128, 128], F32)
    gq = sbg("gq_s", [128, QC], F32)
    gkv = sbg("gkv_s", [128, KC], F32)

    eps5 = sbg("eps5", [1, 1], F32)
    eps6 = sbg("eps6", [1, 1], F32)
    t_ones = P.emit("v", lambda v: v.memset(ones_c[:], 1.0), inc="vs")
    t_eps5 = P.emit("v", lambda v: v.memset(eps5[:], 1e-5), inc="vs")
    t_eps6 = P.emit("v", lambda v: v.memset(eps6[:], 1e-6), inc="vs")
    t_onesr = P.emit("v", lambda v: v.memset(onesr_f[:], 1.0), inc="vs")
    t_h128 = G.dma(lambda g: g.dma_start(h128_b[:], h128_i))
    t_m64 = G.dma(lambda g: g.dma_start(m64_b[:], m64_i))
    t_r64 = G.dma(lambda g: g.dma_start(r64t_b[:], r64t_i))
    t_idenf = Q.dma(lambda q: q.dma_start(idenf[:], idenf_i))
    t_gq = Q.dma(lambda q: q.dma_start(gq[:], gq_i))
    t_gkv = Q.dma(lambda q: q.dma_start(gkv[:], gkv_i))

    stAB = ExitStack()
    sab = lambda n, s, d: stAB.enter_context(nc.sbuf_tensor(n, s, d))
    ckv_fm = sab("ckv_fm", [128, KC * S], BF16)
    kaug2 = sab("kaug2", [DR + 1, S], BF16)
    cq_fm = sab("cq_fm", [128, QC * OWN], BF16)
    dbg_u_sb = sab("dbg_u_sb", [128, KC * S], BF16) if cfg.get("DBG") else None
    t_kones = P.emit("v", lambda v: v.memset(kaug2[DR:DR + 1, :], 1.0), inc="vs")

    stW2 = ExitStack()
    w2in = [stW2.enter_context(nc.sbuf_tensor(f"w2in{i}", [128, HID], BF16)) for i in range(2)]
    w2st = [stW2.enter_context(nc.sbuf_tensor(f"w2st{i}", [128, HID], BF16)) for i in range(2)]
    w2in_r = [Res(), Res()]
    w2st_r = [Res(), Res()]
    stStats = ExitStack()
    stat_muf = [stStats.enter_context(nc.sbuf_tensor(f"stat_muf{i}", [1, 512], F32)) for i in range(2)]
    stat_varf = [stStats.enter_context(nc.sbuf_tensor(f"stat_varf{i}", [1, 512], F32)) for i in range(2)]
    stat_corr = [stStats.enter_context(nc.sbuf_tensor(f"stat_corr{i}", [1, 512], BF16)) for i in range(2)]
    stat_mv_r = [Res(), Res()]
    w2scale = 1.0 / math.sqrt(FFNH)
    NH = HID // 512
    _fold_c = [0]

    def fold_step(nsteps):
        for _ in range(nsteps):
            c = _fold_c[0]
            if c >= FC:
                return
            _fold_c[0] = c + 1
            bi = c % 2
            tin = G.dma(lambda g, c=c, bi=bi: g.dma_start(w2in[bi][:], w2t[c * 128:(c + 1) * 128, :]),
                        waits=w2in_r[bi].take())
            tin = G.dma(lambda g, c=c, bi=bi: g.dma_start(
                w2in[bi][:], w2t_lo[c * 128:(c + 1) * 128, :], accum_op=ALU.add), waits=[tin])
            ev_toks = []
            for nn in range(NH):
                bk = bank[6 + bi]
                def mm(t, bi=bi, nn=nn, bk=bk):
                    return t.matmul(bk.ap[:, :], h128_b[:], w2in[bi][:, nn * 512:(nn + 1) * 512],
                                    start=True, stop=True)
                tk = P.emit("t", mm, waits=[tin, t_h128] + bk.take() + (w2st_r[bi].take() if nn == 0 else []),
                            inc="ts")
                def ev(v, bi=bi, nn=nn, bk=bk):
                    return v.tensor_scalar_mul(w2st[bi][:, nn * 512:(nn + 1) * 512], bk.ap[:, :], w2scale)
                te = P.emit("v", ev, waits=[tk], inc="vs")
                bk.free.append(te)
                w2in_r[bi].free.append(te)
                ev_toks.append(te)
            def wout(q, c=c, bi=bi):
                dst = w2f_s.rearrange("(p c2) d -> p c2 d", c2=CW)[:, c, :]
                return q.dma_start(dst, w2st[bi][:])
            tout = Q.dma(wout, waits=ev_toks)
            w2st_r[bi].free.append(tout)

    def plan_wsum(w_sb, width, wid_full, dst_b, ready):
        toks = []
        nn = 0
        while nn * 512 < width:
            w = min(512, width - nn * 512)
            bk = bank[2]
            war = bk.take()
            for dc in range(DC):
                def mm(t, dc=dc, nn=nn, w=w, bk=bk):
                    return t.matmul(bk.ap[0:1, 0:w], ones_c[:],
                                    w_sb[:, dc * wid_full + nn * 512: dc * wid_full + nn * 512 + w],
                                    start=(dc == 0), stop=(dc == DC - 1))
                tk = P.emit("t", mm, waits=(ready + [t_ones] + war) if dc == 0 else [],
                            inc="ts" if dc == DC - 1 else None)
            def ev(v, nn=nn, w=w, bk=bk):
                return v.tensor_copy(dst_b[0:1, nn * 512:nn * 512 + w], bk.ap[0:1, 0:w])
            te = P.emit("v", ev, waits=[tk], inc="vs")
            bk.free.append(te)
            toks.append(te)
            nn += 1
        tb = P.emit("v", lambda v: v.drain(), waits=toks)
        return P.emit("v", lambda v: v.nop(), waits=toks, inc="vs")

    def plan_rsqrt_row(src_psum_bk, dst_row, scale, eps_ap, eps_tok, waits):
        def ln_(s):
            return s.activation(dst_row[0:1, 0:512], src_psum_bk.ap[0:1, 0:512], ACT.Ln,
                                scale=scale)
        tl = P.emit("s", ln_, waits=list(waits) + [eps_tok], inc="ss")
        src_psum_bk.free.append(tl)
        P.emit("s", lambda s: s.drain())
        def ex_(s):
            return s.activation(dst_row[0:1, 0:512], dst_row[0:1, 0:512], ACT.Exp, scale=-0.5)
        return P.emit("s", ex_, waits=[tl], inc="ss")

    def plan_rep(row_ap, rep_sb, rep_res, nrows, waits):
        bk = bank[5]
        def mm(t, bk=bk):
            return t.matmul(bk.ap[0:nrows, 0:512], onesr_f[0:1, 0:nrows], row_ap, start=True, stop=True)
        tr = P.emit("t", mm, waits=list(waits) + [t_onesr] + bk.take(), inc="ts")
        def ev(v, bk=bk):
            return v.tensor_copy(rep_sb[0:nrows, 0:512], bk.ap[0:nrows, 0:512])
        te = P.emit("v", ev, waits=[tr] + rep_res.take(), inc="vs")
        bk.free.append(te)
        return te

    def stats_mm(xsrc, sqb2buf, sqb2_rs, g, nmu_all, arow_all, off, ready, gbi, rms_w):
        bkM = bank[2]
        bkQ = bank[4]
        warM = bkM.take()
        warQ = bkQ.take()
        sq_toks = []
        for dc in range(DC):
            sb_ = sqb2buf[dc % 2]
            def sq(v, dc=dc, sb_=sb_, xsrc=xsrc):
                return v.tensor_mul(sb_[:], xsrc[:, dc * 512:(dc + 1) * 512],
                                    xsrc[:, dc * 512:(dc + 1) * 512])
            tsq = P.emit("v", sq, waits=([ready] if dc == 0 else []) + sqb2_rs[dc % 2].take(), inc="vs")
            def mmu(t, dc=dc, bkM=bkM, xsrc=xsrc):
                return t.matmul(bkM.ap[0:1, 0:512], ones_c[:], xsrc[:, dc * 512:(dc + 1) * 512],
                                start=(dc == 0), stop=(dc == DC - 1))
            P.emit("t", mmu, waits=([ready, t_ones] + warM) if dc == 0 else [])
            def msq(t, dc=dc, bkQ=bkQ, sb_=sb_):
                return t.matmul(bkQ.ap[0:1, 0:512], ones_c[:], sb_[:],
                                start=(dc == 0), stop=(dc == DC - 1))
            tms = P.emit("t", msq, waits=[tsq] + (warQ if dc == 0 else []), inc="ts")
            sqb2_rs[dc % 2].free.append(tms)
        muf = stat_muf[gbi]
        varf = stat_varf[gbi]
        mwar = stat_mv_r[gbi].take()
        def r0(v, bkM=bkM, muf=muf):
            return v.tensor_scalar_mul(muf[0:1, :], bkM.ap[0:1, 0:512], 1.0 / HID)
        t0 = P.emit("v", r0, waits=[tms] + mwar, inc="vs")
        bkM.free.append(t0)
        def nm(v, muf=muf, off=off):
            return v.tensor_scalar_mul(nmu_all[0:1, off:off + 512], muf[0:1, :], -1.0)
        tnm = P.emit("v", nm, waits=[t0], inc="vs")
        P.emit("v", lambda v: v.drain())
        def m2(v, muf=muf, varf=varf):
            return v.tensor_mul(varf[0:1, :], muf[0:1, :], muf[0:1, :])
        t2 = P.emit("v", m2, waits=[t0], inc="vs")
        P.emit("v", lambda v: v.drain())
        stat_mv_r[gbi].free.append(tnm)
        def vr(v, bkQ=bkQ, varf=varf):
            return v.scalar_tensor_tensor(varf[0:1, :], bkQ.ap[0:1, 0:512], 1.0 / HID,
                                          varf[0:1, :], ALU.mult, ALU.subtract)
        tv = P.emit("v", vr, waits=[t2], inc="vs")
        P.emit("v", lambda v: v.drain())
        bkQ.free.append(tv)
        def cr(v, varf=varf, gbi=gbi):
            return v.tensor_scalar(stat_corr[gbi][0:1, :], varf[0:1, :], 1e-5,
                                   rms_w * 1e-6, ALU.add, ALU.mult)
        tcr = P.emit("v", cr, waits=[tv], inc="vs")
        P.emit("v", lambda v: v.drain())
        ta = None
        if arow_all is not None:
            def ln_(s, varf=varf, off=off):
                return s.activation(arow_all[0:1, off:off + 512], varf[0:1, :], ACT.Ln,
                                    bias=eps5[0:1, 0:1], scale=1.0)
            tl = P.emit("s", ln_, waits=[tv, t_eps5], inc="ss")
            P.emit("s", lambda s: s.drain())
            def ex_(s, off=off):
                return s.activation(arow_all[0:1, off:off + 512],
                                    arow_all[0:1, off:off + 512], ACT.Exp, scale=-0.5)
            ta = P.emit("s", ex_, waits=[tl], inc="ss")
            stat_mv_r[gbi].free.append(ta)
        return tnm, ta, tcr

    stA1 = ExitStack()
    sa1 = lambda n, s, d: stA1.enter_context(nc.sbuf_tensor(n, s, d))
    wkva_b = sa1("wkva_b", [128, DC * KVW], BF16)
    wsumkv_b = sa1("wsumkv_b", [1, KVW], BF16)
    xbf = [sa1("xbf0", [128, DC * 512], BF16), sa1("xbf1", [128, DC * 512], BF16)]
    xbf_r = [Res(), Res()]
    nmu_all = sa1("nmu_all", [1, S], BF16)
    arow_all = sa1("arow_all", [1, S], F32)
    arep_s = sa1("arep_s", [DR, 512], F32)
    arep_r = Res()
    sqb = sa1("sqb1", [128, 512], BF16)
    sqb_r = Res()
    sqbp = [sa1("sqbp0", [128, 512], BF16), sa1("sqbp1", [128, 512], BF16)]
    sqbp_r = [Res(), Res()]
    kp_b = sa1("kp_b", [DR, 512], BF16)
    kp_r = Res()
    rope1 = sa1("rope1", [DR, 512], BF16)
    rope1_r = Res()
    rope2 = sa1("rope2", [DR, 512], BF16)
    rope2_r = Res()
    rrow = sa1("rrow1", [1, 512], F32)
    rrow_r = Res()
    rrep = sa1("rrep1", [128, 512], BF16)
    rrep_r = Res()
    coss = [sa1("coss0", [DR, 512], F32)]
    coss_r = [Res()]
    sins = [sa1("sins0", [DR, 512], F32)]
    sins_r = [Res()]

    tw_kva = G.dma(lambda g: g.dma_start(wkva_b[:], wkvaT.rearrange("(c p) q -> p c q", p=128)))
    tw_kva = G.dma(lambda g: g.dma_start(wkva_b[:], wkvaT_lo.rearrange("(c p) q -> p c q", p=128),
                                         accum_op=ALU.add), waits=[tw_kva])
    if cfg.get("DBG"):
        dbg_wkva2 = dram("dbg_wkva2", [128, DC * KVW], BF16, N)
        Q.dma(lambda q: q.dma_start(dbg_wkva2, wkva_b[:]), waits=[tw_kva])
    t_wsumkv = plan_wsum(wkva_b, KVW, KVW, wsumkv_b, [tw_kva])

    ckv_ready = []
    pe_ready = []
    ka_ready = [t_kones]
    xb_toks = {0: G.dma(lambda g: g.dma_start(
        xbf[0][:], xf.rearrange("(c p) t -> p c t", p=128)[:, :, 0:512]),
        waits=xbf_r[0].take())}
    for g0 in range(NG):
        bi = g0 % 2
        txb = xb_toks.pop(g0)
        if g0 + 1 < NG:
            xb_toks[g0 + 1] = G.dma(lambda g, g0=g0: g.dma_start(
                xbf[(g0 + 1) % 2][:],
                xf.rearrange("(c p) t -> p c t", p=128)[:, :, (g0 + 1) * 512:(g0 + 2) * 512]),
                waits=xbf_r[(g0 + 1) % 2].take())
        fold_step((FC // 2 + NG - 1) // NG)
        tc = Q.dma(lambda q, g0=g0: q.dma_start(coss[0][:], cosf_i[:, g0 * 512:(g0 + 1) * 512]),
                   waits=coss_r[0].take())
        tc = G.dma(lambda g, g0=g0: g.dma_start(coss[0][:], cosf_lo[:, g0 * 512:(g0 + 1) * 512],
                                                accum_op=ALU.add), waits=[tc])
        tsn = Q.dma(lambda q, g0=g0: q.dma_start(sins[0][:], sinf_i[:, g0 * 512:(g0 + 1) * 512]),
                    waits=sins_r[0].take())
        tsn = G.dma(lambda g, g0=g0: g.dma_start(sins[0][:], sinf_lo[:, g0 * 512:(g0 + 1) * 512],
                                                 accum_op=ALU.add), waits=[tsn])
        if g0 == 0:
            st_next = stats_mm(xbf[0], sqbp, sqbp_r, 0, nmu_all, arow_all, 0, txb, 0, KVLR)
        tnm, ta, tcr = st_next
        if g0 + 1 < NG:
            st_next = stats_mm(xbf[(g0 + 1) % 2], sqbp, sqbp_r, g0 + 1, nmu_all, arow_all,
                               (g0 + 1) * 512, xb_toks[g0 + 1], (g0 + 1) % 2, KVLR)
        trpe = plan_rep(arow_all[0:1, g0 * 512:(g0 + 1) * 512], arep_s, arep_r, DR, [ta])

        col0 = 0
        tnrm = None
        for mc in range(KC + 1):
            mw = KVMC[mc]
            bk = bank[mc % 2]
            war = bk.take()
            for dc in range(DC):
                def mm(t, mc=mc, dc=dc, bi=bi, mw=mw, col0=col0, bk=bk):
                    return t.matmul(bk.ap[0:mw, 0:512],
                                    wkva_b[:, dc * KVW + col0: dc * KVW + col0 + mw],
                                    xbf[bi][:, dc * 512:(dc + 1) * 512],
                                    start=(dc == 0), stop=False)
                P.emit("t", mm, waits=([txb, tw_kva] + war) if dc == 0 else [])
            def fold(t, mc=mc, mw=mw, col0=col0, bk=bk, g0=g0):
                return t.matmul(bk.ap[0:mw, 0:512], wsumkv_b[0:1, col0:col0 + mw],
                                nmu_all[0:1, g0 * 512:(g0 + 1) * 512], start=False, stop=True)
            tmm = P.emit("t", fold, waits=[t_wsumkv, tnm], inc="ts")
            if mc < KC:
                def ev(v, mc=mc, g0=g0, bk=bk):
                    return v.tensor_copy(ckv_fm[:, mc * S + g0 * 512: mc * S + (g0 + 1) * 512],
                                         bk.ap[:, :])
                tev = P.emit("v", ev, waits=[tmm], inc="vs")
                bk.free.append(tev)
                if cfg.get("DBG"):
                    def dcp(v, mc=mc, g0=g0):
                        sl = slice(mc * S + g0 * 512, mc * S + (g0 + 1) * 512)
                        return v.tensor_copy(dbg_u_sb[:, sl], ckv_fm[:, sl])
                    P.emit("v", dcp, waits=[tev], inc="vs")
                def sq(v, mc=mc, g0=g0):
                    sl = ckv_fm[:, mc * S + g0 * 512: mc * S + (g0 + 1) * 512]
                    return v.tensor_mul(sqb[:], sl, sl)
                tsq = P.emit("v", sq, waits=[tev] + sqb_r.take(), inc="vs")
                bkN = bank[3]
                if mc == 0:
                    def cmm(t, bkN=bkN, g0=g0):
                        return t.matmul(bkN.ap[0:1, 0:512], ones_c[0:1, 0:1],
                                        stat_corr[g0 % 2][0:1, :], start=True, stop=False)
                    P.emit("t", cmm, waits=[tcr, t_ones] + bkN.take())
                def nmm(t, mc=mc, bkN=bkN):
                    return t.matmul(bkN.ap[0:1, 0:512], ones_c[:], sqb[:],
                                    start=False, stop=(mc == KC - 1))
                tnrm = P.emit("t", nmm, waits=[tsq], inc="ts")
                sqb_r.free.append(tnrm)
                if mc == KC - 1:
                    stat_mv_r[g0 % 2].free.append(tnrm)
            else:
                def pe(v, mw=mw, bk=bk):
                    return v.scalar_tensor_tensor(kp_b[:], bk.ap[0:DR, 0:512], 1.0,
                                                  arep_s[0:DR, :], ALU.mult, ALU.mult)
                tpe = P.emit("v", pe, waits=[tmm, trpe] + kp_r.take(), inc="vs")
                bk.free.append(tpe)
                arep_r.free.append(tpe)
            col0 += mw
        xbf_r[bi].free.append(P.tok("ts"))

        trs = plan_rsqrt_row(bank[3], rrow, 1.0 / KVLR, eps6, t_eps6, [tnrm] + rrow_r.take())
        trep = plan_rep(rrow[0:1, 0:512], rrep, rrep_r, 128, [trs])
        rrow_r.free.append(P.tok("ts"))
        for mc in range(KC):
            def ap_(v, mc=mc, g0=g0):
                sl = ckv_fm[:, mc * S + g0 * 512: mc * S + (g0 + 1) * 512]
                return v.scalar_tensor_tensor(sl, sl, gkv[:, mc:mc + 1], rrep[:], ALU.mult, ALU.mult)
            tap = P.emit("v", ap_, waits=[trep, t_gkv], inc="vs")
            P.emit("v", lambda v: v.drain())
            rrep_r.free.append(tap)
            ckv_ready.append(tap)

        bkR = bank[4]
        def rot(t, bkR=bkR):
            return t.matmul(bkR.ap[0:DR, 0:512], r64t_b[:], kp_b[:], start=True, stop=True)
        trot = P.emit("t", rot, waits=[tpe, t_r64] + bkR.take(), inc="ts")
        def rc(v):
            return v.tensor_mul(rope1[:], kp_b[:], coss[0][:])
        t1 = P.emit("v", rc, waits=[tpe, tc] + rope1_r.take(), inc="vs")
        kp_r.free.append(trot)
        def rs_(v, bkR=bkR):
            return v.scalar_tensor_tensor(rope2[:], bkR.ap[0:DR, 0:512], 1.0,
                                          sins[0][:], ALU.mult, ALU.mult)
        t2 = P.emit("v", rs_, waits=[trot, tsn] + rope2_r.take(), inc="vs")
        bkR.free.append(t2)
        def radd(v, g0=g0):
            return v.tensor_add(kaug2[0:DR, g0 * 512:(g0 + 1) * 512], rope1[:], rope2[:])
        t3 = P.emit("v", radd, waits=[t1, t2], inc="vs")
        rope1_r.free.append(t3)
        rope2_r.free.append(t3)
        kp_r.free.append(t3)
        coss_r[0].free.append(t3)
        sins_r[0].free.append(t3)
        ka_ready.append(t3)
        pe_ready.append(t3)

    barrier()
    if cfg.get("DBG"):
        dbg_xbf = dram("dbg_xbf", [128, DC * 512], BF16, N)
        dbg_wkva = dram("dbg_wkva", [128, DC * KVW], BF16, N)
        Q.dma(lambda q: q.dma_start(dbg_xbf, xbf[(S // 512 - 1) % 2][:]))
        Q.dma(lambda q: q.dma_start(dbg_wkva, wkva_b[:]))
        barrier()
    stA1.close()
    stA2 = ExitStack()
    sa2 = lambda n, s, d: stA2.enter_context(nc.sbuf_tensor(n, s, d))
    wqa_b = sa2("wqa_b", [128, DC * QLR], BF16)
    wsumq_b = sa2("wsumq_b", [1, QLR], BF16)
    xbf2 = [sa2("xbf2a", [128, DC * 512], BF16), sa2("xbf2b", [128, DC * 512], BF16)]
    xbf2_r = [Res(), Res()]
    sqbp2 = [sa2("sqbp2a", [128, 512], BF16), sa2("sqbp2b", [128, 512], BF16)]
    sqbp2_r = [Res(), Res()]
    sqb2 = sqbp2[0]
    sqb2_r = sqbp2_r[0]
    nmu2_all = sa2("nmu2_all", [1, OWN], BF16)
    rrow2 = sa2("rrow2", [1, 512], F32)
    rrow2_r = Res()
    rrep2 = sa2("rrep2", [128, 512], BF16)
    rrep2_r = Res()

    tw_qa = G.dma(lambda g: g.dma_start(wqa_b[:], wqaT.rearrange("(c p) q -> p c q", p=128)))
    tw_qa = G.dma(lambda g: g.dma_start(wqa_b[:], wqaT_lo.rearrange("(c p) q -> p c q", p=128),
                                        accum_op=ALU.add), waits=[tw_qa])
    t_wsumq = plan_wsum(wqa_b, QLR, QLR, wsumq_b, [tw_qa])

    cq_ready = []
    xb2_toks = {0: G.dma(lambda g: g.dma_start(
        xbf2[0][:], xof.rearrange("(c p) t -> p c t", p=128)[:, :, 0:512]),
        waits=xbf2_r[0].take())}
    for og in range(NBLK):
        bi = og % 2
        txb = xb2_toks.pop(og)
        if og + 1 < NBLK:
            xb2_toks[og + 1] = G.dma(lambda g, og=og: g.dma_start(
                xbf2[(og + 1) % 2][:],
                xof.rearrange("(c p) t -> p c t", p=128)[:, :, (og + 1) * 512:(og + 2) * 512]),
                waits=xbf2_r[(og + 1) % 2].take())
        fold_step((FC // 2 + NBLK - 1) // NBLK)
        if og == 0:
            st2_next = stats_mm(xbf2[0], sqbp2, sqbp2_r, 0, nmu2_all, None, 0, txb, 0, QLR)
        tnm, _, tcr = st2_next
        if og + 1 < NBLK:
            st2_next = stats_mm(xbf2[(og + 1) % 2], sqbp2, sqbp2_r, og + 1, nmu2_all, None,
                                   (og + 1) * 512, xb2_toks[og + 1], (og + 1) % 2, QLR)
        tnrm = None
        for mc in range(QC):
            bk = bank[mc % 2]
            war = bk.take()
            for dc in range(DC):
                def mm(t, mc=mc, dc=dc, bi=bi, bk=bk):
                    return t.matmul(bk.ap[:, 0:512],
                                    wqa_b[:, dc * QLR + mc * 128: dc * QLR + (mc + 1) * 128],
                                    xbf2[bi][:, dc * 512:(dc + 1) * 512],
                                    start=(dc == 0), stop=False)
                P.emit("t", mm, waits=([txb, tw_qa] + war) if dc == 0 else [])
            def fold(t, mc=mc, bk=bk, og=og):
                return t.matmul(bk.ap[:, 0:512], wsumq_b[0:1, mc * 128:(mc + 1) * 128],
                                nmu2_all[0:1, og * 512:(og + 1) * 512], start=False, stop=True)
            tmm = P.emit("t", fold, waits=[t_wsumq, tnm], inc="ts")
            def ev(v, mc=mc, og=og, bk=bk):
                return v.tensor_copy(cq_fm[:, mc * OWN + og * 512: mc * OWN + (og + 1) * 512],
                                     bk.ap[:, :])
            tev = P.emit("v", ev, waits=[tmm], inc="vs")
            bk.free.append(tev)
            def sq(v, mc=mc, og=og):
                sl = cq_fm[:, mc * OWN + og * 512: mc * OWN + (og + 1) * 512]
                return v.tensor_mul(sqb2[:], sl, sl)
            tsq = P.emit("v", sq, waits=[tev] + sqb2_r.take(), inc="vs")
            bkN = bank[3]
            if mc == 0:
                def cmm(t, bkN=bkN, og=og):
                    return t.matmul(bkN.ap[0:1, 0:512], ones_c[0:1, 0:1],
                                    stat_corr[og % 2][0:1, :], start=True, stop=False)
                P.emit("t", cmm, waits=[tcr, t_ones] + bkN.take())
            def nmm(t, mc=mc, bkN=bkN):
                return t.matmul(bkN.ap[0:1, 0:512], ones_c[:], sqb2[:],
                                start=False, stop=(mc == QC - 1))
            tnrm = P.emit("t", nmm, waits=[tsq], inc="ts")
            sqb2_r.free.append(tnrm)
            if mc == QC - 1:
                stat_mv_r[og % 2].free.append(tnrm)
        xbf2_r[bi].free.append(P.tok("ts"))
        trs = plan_rsqrt_row(bank[3], rrow2, 1.0 / QLR, eps6, t_eps6, [tnrm] + rrow2_r.take())
        trep = plan_rep(rrow2[0:1, 0:512], rrep2, rrep2_r, 128, [trs])
        rrow2_r.free.append(P.tok("ts"))
        for mc in range(QC):
            def ap_(v, mc=mc, og=og):
                sl = cq_fm[:, mc * OWN + og * 512: mc * OWN + (og + 1) * 512]
                return v.scalar_tensor_tensor(sl, sl, gq[:, mc:mc + 1], rrep2[:], ALU.mult, ALU.mult)
            tap = P.emit("v", ap_, waits=[trep, t_gq], inc="vs")
            P.emit("v", lambda v: v.drain())
            rrep2_r.free.append(tap)
            cq_ready.append(tap)

    fold_step(FC)
    w2f_done = Q.all_done()

    barrier()
    if cfg.get("DBG"):
        dbg_ckv = dram("dbg_ckv", [128, KC * S], BF16, N)
        dbg_kaug = dram("dbg_kaug", [DR + 1, S], BF16, N)
        dbg_cq = dram("dbg_cq", [128, QC * OWN], BF16, N)
        dbg_u = dram("dbg_u", [128, KC * S], BF16, N)
        Q.dma(lambda q: q.dma_start(dbg_ckv, ckv_fm[:]))
        Q.dma(lambda q: q.dma_start(dbg_kaug, kaug2[:]))
        Q.dma(lambda q: q.dma_start(dbg_cq, cq_fm[:]))
        Q.dma(lambda q: q.dma_start(dbg_u, dbg_u_sb[:]))
    stA2.close()
    stStats.close()
    stW2.close()
    stB = ExitStack()
    sb = lambda n, s, d: stB.enter_context(nc.sbuf_tensor(n, s, d))
    wqb_b = [sb(f"wqb_b{i}", [128, QC * 2 * DQ], BF16) for i in range(2)]
    wqb_r = [Res(), Res()]
    wkvb_b = [sb(f"wkvb_b{i}", [128, KC * KVWID], BF16) for i in range(2)]
    wkvb_r = [Res(), Res()]
    kn_b = [sb(f"kn_b{i}", [128, S], BF16) for i in range(2)]
    kn_r = [Res(), Res()]
    v_b = sb("v_b", [128, SEQT * 2 * DV], BF16)
    v_r = Res()
    qn_b = sb("qn_b", [128, OWN], BF16)
    qn_r = Res()
    qa2_b = sb("qa2_b", [DR + 1, OWN], BF16)
    qa2_r = Res()
    qpe_raw = sb("qpe_raw", [DR, 512], BF16)
    qpe_r = Res()
    ropeb1 = sb("ropeb1", [DR, 512], F32)
    ropeb1_r = Res()
    ropeb2 = sb("ropeb2", [DR, 512], F32)
    ropeb2_r = Res()
    pt_b = [sb(f"pt_b{i}", [128, 512], BF16) for i in range(3)]
    pt_r = [Res(), Res(), Res()]
    STB = (2, 3, 7)
    kn2row = [sb(f"kn2row{i}", [1, S], BF16) for i in range(2)]
    sq64b = [sb("sq64bA", [DR, 512], BF16), sb("sq64bB", [DR, 512], BF16)]
    sq64b_r = [Res(), Res()]
    kn2_r = [Res(), Res()]
    m2row = sb("m2row", [1, NBLK], F32)
    m2_r = Res()
    lnm2 = sb("lnm2", [1, NBLK], F32)
    lnm2_r = Res()
    lnq_row = sb("lnq_row", [1, 512], F32)
    lnq_r = Res()
    mrow_f = sb("mrow_f", [1, 512], F32)
    mrow_r = Res()
    rs_row = sb("rs_row", [1, 512], F32)
    rs_r = Res()
    rep_ct = sb("rep_ct", [128, 512], F32)
    repct_r = Res()
    ctx_st = sb("ctx_st", [128, 512], BF16)
    ctxst_r = Res()
    sqq = [sb("sqqA", [128, 512], BF16), sb("sqqB", [128, 512], BF16)]
    sqq_r = [Res(), Res()]
    coso = [sb(f"coso{i}", [DR, 512], F32) for i in range(2)]
    coso_r = [Res(), Res()]
    sino = [sb(f"sino{i}", [DR, 512], F32) for i in range(2)]
    sino_r = [Res(), Res()]

    mask_b = [sb(f"maskb{kd}", [128, 512 - 128 * (kd // 2)], BF16) for kd in range(8)]
    t_masks = [G.dma(lambda g, kd=kd: g.dma_start(mask_b[kd][:], mask_i[kd])) for kd in range(8)]
    cqk = _KList(cq_ready)
    kak = _KList(ka_ready)
    pend_fin = [None]
    pend_tail = [None]
    for hp in range(NPAIR):
        if pend_fin[0] is not None:
            pend_fin[0]()
            pend_fin[0] = None
        pbi = hp % 2
        twq = G.dma(lambda g, hp=hp, pbi=pbi: g.dma_start(
            wqb_b[pbi][:],
            wqbT.rearrange("(c p) q -> p c q", p=128)[:, :, hp * 2 * DQ:(hp + 1) * 2 * DQ]),
            waits=wqb_r[pbi].take())
        twq = G.dma(lambda g, hp=hp, pbi=pbi: g.dma_start(
            wqb_b[pbi][:],
            wqbT_lo.rearrange("(c p) q -> p c q", p=128)[:, :, hp * 2 * DQ:(hp + 1) * 2 * DQ],
            accum_op=ALU.add), waits=[twq])
        twk = G.dma(lambda g, hp=hp, pbi=pbi: g.dma_start(
            wkvb_b[pbi][:],
            wkvbT.rearrange("(c p) q -> p c q", p=128)[:, :, hp * KVWID:(hp + 1) * KVWID]),
            waits=wkvb_r[pbi].take())
        twk = G.dma(lambda g, hp=hp, pbi=pbi: g.dma_start(
            wkvb_b[pbi][:],
            wkvbT_lo.rearrange("(c p) q -> p c q", p=128)[:, :, hp * KVWID:(hp + 1) * KVWID],
            accum_op=ALU.add), waits=[twk])

        for hl in range(2):
            kcol = hl * (DN + DV)
            knwar = kn_r[hl].take()
            kn2war = kn2_r[hl].take()
            sqs_toks = {}

            def knorm_mms(nn, hl=hl, kn2war=kn2war):
                tsq, tspe = sqs_toks.pop(nn)
                bkN = bank[3]
                def nmm(t, nn=nn, bkN=bkN):
                    return t.matmul(bkN.ap[0:1, 0:512], ones_c[:], sqq[nn % 2][:],
                                    start=True, stop=False)
                tn1 = P.emit("t", nmm, waits=[tsq] + bkN.take(), inc="ts")
                sqq_r[nn % 2].free.append(tn1)
                def nmm2(t, nn=nn, bkN=bkN):
                    return t.matmul(bkN.ap[0:1, 0:512], ones_c[0:DR, :], sq64b[nn % 2][:],
                                    start=False, stop=True)
                tnm_ = P.emit("t", nmm2, waits=[tspe], inc="ts")
                sq64b_r[nn % 2].free.append(tnm_)
                def evk(v, hl=hl, nn=nn, bkN=bkN):
                    return v.tensor_copy(kn2row[hl][0:1, nn * 512:(nn + 1) * 512],
                                        bkN.ap[0:1, 0:512])
                tad = P.emit("v", evk, waits=[tnm_] + (kn2war if nn == 0 else []), inc="vs")
                bkN.free.append(tad)
                return tad

            for nn in range(S // 512):
                bk = bank[6 + nn % 2]
                war = bk.take() + (knwar if nn == 0 else [])
                for kc in range(KC):
                    def mm(t, hl=hl, nn=nn, kc=kc, pbi=pbi, kcol=kcol, bk=bk):
                        return t.matmul(bk.ap[:, :],
                                        wkvb_b[pbi][:, kc * KVWID + kcol: kc * KVWID + kcol + DN],
                                        ckv_fm[:, kc * S + nn * 512: kc * S + (nn + 1) * 512],
                                        start=(kc == 0), stop=(kc == KC - 1))
                    tk = P.emit("t", mm, waits=([twk] + ckv_ready + war) if kc == 0 else [],
                                inc="ts" if kc == KC - 1 else None)
                ckv_ready = []
                def ev(v, hl=hl, nn=nn, bk=bk):
                    return v.tensor_scalar_mul(kn_b[hl][:, nn * 512:(nn + 1) * 512], bk.ap[:, :], S4)
                te = P.emit("v", ev, waits=[tk], inc="vs")
                bk.free.append(te)
                def sq(v, hl=hl, nn=nn):
                    sl = kn_b[hl][:, nn * 512:(nn + 1) * 512]
                    return v.tensor_mul(sqq[nn % 2][:], sl, sl)
                tsq = P.emit("v", sq, waits=[te] + sqq_r[nn % 2].take(), inc="vs")
                def sqpe(v, nn=nn):
                    sl = kaug2[0:DR, nn * 512:(nn + 1) * 512]
                    return v.tensor_mul(sq64b[nn % 2][:], sl, sl)
                tspe = P.emit("v", sqpe, waits=pe_ready + sq64b_r[nn % 2].take(), inc="vs")
                pe_ready = []
                sqs_toks[nn] = (tsq, tspe)
                if nn > 0:
                    knorm_mms(nn - 1)
            kn2_done = knorm_mms((S // 512) - 1)

        vwar = v_r.take()
        for kt in range(SEQT):
            bk = bank[6 + kt % 2]
            war = bk.take() + (vwar if kt == 0 else [])
            for kc in range(KC):
                def mm(t, kt=kt, kc=kc, pbi=pbi, bk=bk):
                    wv = wkvb_b[pbi][:].rearrange("p (c h d) -> p c h d", h=2, d=DN + DV)
                    return t.matmul(bk.ap[:, 0:2 * DV],
                                    ckv_fm[:, kc * S + kt * 128: kc * S + (kt + 1) * 128],
                                    wv[:, kc, :, DN:DN + DV],
                                    start=(kc == 0), stop=(kc == KC - 1))
                tk = P.emit("t", mm, waits=([twk] + war) if kc == 0 else [],
                            inc="ts" if kc == KC - 1 else None)
            def ev(v, kt=kt, bk=bk):
                return v.tensor_copy(v_b[:, kt * 2 * DV:(kt + 1) * 2 * DV], bk.ap[:, 0:2 * DV])
            te = P.emit("v", ev, waits=[tk], inc="vs")
            bk.free.append(te)
        v_done = te

        for hl in range(2):
            h = hp * 2 + hl
            qcol = hl * DQ
            m2war = m2_r.take() + lnm2_r.take()
            prev = None
            for j in range(NBLK):
                def rm(v, j=j, hl=hl):
                    return v.reduce_max(m2row[0:1, j:j + 1], kn2row[hl][0:1, j * 1024:(j + 1) * 1024],
                                        axis=AX.X)
                tm = P.emit("v", rm, waits=[kn2_done] + (m2war if j == 0 else []), inc="vs")
                P.emit("v", lambda v: v.drain())
                if j > 0:
                    def cmb(v, j=j):
                        return v.tensor_max(m2row[0:1, j:j + 1], m2row[0:1, j:j + 1], m2row[0:1, j - 1:j])
                    tm = P.emit("v", cmb, inc="vs")
                    P.emit("v", lambda v: v.drain())
            def lnm(s):
                return s.activation(lnm2[0:1, :], m2row[0:1, :], ACT.Ln)
            tlnm = P.emit("s", lnm, waits=[tm], inc="ss")
            m2_r.free.append(tlnm)

            qn_war = qn_r.take()
            qa2_war = qa2_r.take()

            def emit_prep(og, hl=hl, pbi=pbi, qcol=qcol, tlnm=tlnm):
                cbi = og % 2
                tc = Q.dma(lambda q, og=og, cbi=cbi: q.dma_start(
                    coso[cbi][:], cosfo_i[:, og * 512:(og + 1) * 512]), waits=coso_r[cbi].take())
                tc = G.dma(lambda g, og=og, cbi=cbi: g.dma_start(
                    coso[cbi][:], cosfo_lo[:, og * 512:(og + 1) * 512],
                    accum_op=ALU.add), waits=[tc])
                tsn = Q.dma(lambda q, og=og, cbi=cbi: q.dma_start(
                    sino[cbi][:], sinfo_i[:, og * 512:(og + 1) * 512]), waits=sino_r[cbi].take())
                tsn = G.dma(lambda g, og=og, cbi=cbi: g.dma_start(
                    sino[cbi][:], sinfo_lo[:, og * 512:(og + 1) * 512],
                    accum_op=ALU.add), waits=[tsn])
                bk = bank[0]
                war = bk.take() + (qn_war if og == 0 else [])
                for qc in range(QC):
                    def mm(t, og=og, qc=qc, pbi=pbi, qcol=qcol, bk=bk):
                        return t.matmul(bk.ap[:, :],
                                        wqb_b[pbi][:, qc * 2 * DQ + qcol: qc * 2 * DQ + qcol + DN],
                                        cq_fm[:, qc * OWN + og * 512: qc * OWN + (og + 1) * 512],
                                        start=(qc == 0), stop=(qc == QC - 1))
                    tk = P.emit("t", mm, waits=([twq] + cqk.pop() + war) if qc == 0 else [],
                                inc="ts" if qc == QC - 1 else None)
                def ev(v, og=og, bk=bk):
                    return v.tensor_scalar_mul(qn_b[:, og * 512:(og + 1) * 512], bk.ap[:, :], S4)
                te = P.emit("v", ev, waits=[tk], inc="vs")
                bk.free.append(te)
                bk1 = bank[1]
                war1 = bk1.take()
                for qc in range(QC):
                    def mm(t, og=og, qc=qc, pbi=pbi, qcol=qcol, bk1=bk1):
                        return t.matmul(bk1.ap[0:DR, 0:512],
                                        wqb_b[pbi][:, qc * 2 * DQ + qcol + DN: qc * 2 * DQ + qcol + DQ],
                                        cq_fm[:, qc * OWN + og * 512: qc * OWN + (og + 1) * 512],
                                        start=(qc == 0), stop=(qc == QC - 1))
                    tk1 = P.emit("t", mm, waits=war1 if qc == 0 else [],
                                 inc="ts" if qc == QC - 1 else None)
                def sq(v, og=og):
                    sl = qn_b[:, og * 512:(og + 1) * 512]
                    return v.tensor_mul(sqq[0][:], sl, sl)
                tsq = P.emit("v", sq, waits=[te] + sqq_r[0].take(), inc="vs")
                bkN = bank[0]
                def nmm(t, bkN=bkN):
                    return t.matmul(bkN.ap[0:1, 0:512], ones_c[:], sqq[0][:], start=True, stop=False)
                tqn1 = P.emit("t", nmm, waits=[tsq] + bkN.take(), inc="ts")
                sqq_r[0].free.append(tqn1)
                def evp(v, bk1=bk1):
                    return v.tensor_copy(qpe_raw[:], bk1.ap[0:DR, 0:512])
                tep = P.emit("v", evp, waits=[tk1] + qpe_r.take(), inc="vs")
                bk1.free.append(tep)
                bkR = bank[1]
                def rot(t, bkR=bkR):
                    return t.matmul(bkR.ap[0:DR, 0:512], r64t_b[:], qpe_raw[:], start=True, stop=True)
                trot = P.emit("t", rot, waits=[tep, t_r64] + bkR.take(), inc="ts")
                qpe_r.free.append(trot)
                def rc(v, cbi=cbi):
                    return v.tensor_mul(ropeb1[:], qpe_raw[:], coso[cbi][:])
                t1 = P.emit("v", rc, waits=[tep, tc] + ropeb1_r.take(), inc="vs")
                qpe_r.free.append(t1)
                def rs_(v, cbi=cbi, bkR=bkR):
                    return v.scalar_tensor_tensor(ropeb2[:], bkR.ap[0:DR, 0:512], 1.0,
                                                  sino[cbi][:], ALU.mult, ALU.mult)
                t2 = P.emit("v", rs_, waits=[trot, tsn] + ropeb2_r.take(), inc="vs")
                bkR.free.append(t2)
                def radd(v, og=og):
                    return v.tensor_add(qa2_b[0:DR, og * 512:(og + 1) * 512], ropeb1[:], ropeb2[:])
                t3 = P.emit("v", radd, waits=[t1, t2] + (qa2_war if og == 0 else []), inc="vs")
                ropeb1_r.free.append(t3)
                ropeb2_r.free.append(t3)
                coso_r[cbi].free.append(t3)
                sino_r[cbi].free.append(t3)
                def tail(og=og, bkN=bkN, t3=t3, cbi=cbi):
                    return prep_tail(og, bkN, t3, cbi)
                return te, tail

            def prep_tail(og, bkN, t3, cbi, hl=hl, tlnm=tlnm):
                def sqp(v, og=og):
                    sl = qa2_b[0:DR, og * 512:(og + 1) * 512]
                    return v.tensor_mul(sqq[0][0:DR, :], sl, sl)
                tqp = P.emit("v", sqp, waits=[t3] + sqq_r[0].take(), inc="vs")
                def nmm2(t, bkN=bkN):
                    return t.matmul(bkN.ap[0:1, 0:512], ones_c[0:DR, :], sqq[0][0:DR, :],
                                    start=False, stop=True)
                tq2 = P.emit("t", nmm2, waits=[tqp], inc="ts")
                sqq_r[0].free.append(tq2)
                def lnq(s, bkN=bkN):
                    return s.activation(lnq_row[0:1, 0:512], bkN.ap[0:1, 0:512], ACT.Ln)
                tlq = P.emit("s", lnq, waits=[tq2, tlnm] + lnq_r.take(), inc="ss")
                bkN.free.append(tlq)
                def addm(v, og=og):
                    return v.tensor_scalar(mrow_f[0:1, 0:512], lnq_row[0:1, 0:512],
                                           lnm2[0:1, og:og + 1], None, ALU.add)
                tam = P.emit("v", addm, waits=[tlq] + mrow_r.take(), inc="vs")
                lnq_r.free.append(tam)
                lnm2_r.free.append(tam)
                def exm(s):
                    return s.activation(mrow_f[0:1, 0:512], mrow_f[0:1, 0:512], ACT.Exp, scale=0.5)
                tem = P.emit("s", exm, waits=[tam], inc="ss")
                def negm(v, og=og):
                    return v.tensor_scalar_mul(qa2_b[DR:DR + 1, og * 512:(og + 1) * 512],
                                               mrow_f[0:1, 0:512], -1.0)
                tnegm = P.emit("v", negm, waits=[tem], inc="vs")
                mrow_r.free.append(tnegm)
                return tnegm

            def emit_attn(og, te, tnegm, hl=hl):
                bkCT = bank[4 + 2 * (og % 2)]
                bkS = bank[5]
                kts = list(range(8 * og)) + [("d", kd) for kd in range(8)]
                nkt = len(kts)

                def _geo(item):
                    if isinstance(item, tuple):
                        kd = item[1]
                        return 8 * og + kd, kd, 512 - 128 * (kd // 2), 128 * (kd // 2)
                    return item, None, 512, 0

                def emit_scores(item, idx, hl=hl, og=og, te=te, v_done=v_done, tnegm=tnegm):
                    kt, kd, W, coff = _geo(item)
                    sbk = bank[STB[idx % 3]]
                    swar = sbk.take()
                    def mm1(t, hl=hl, kt=kt, og=og, W=W, coff=coff, sbk=sbk):
                        return t.matmul(sbk.ap[:, 0:W],
                                        kn_b[hl][:, kt * 128:(kt + 1) * 128],
                                        qn_b[:, og * 512 + coff: og * 512 + coff + W],
                                        start=True, stop=False)
                    P.emit("t", mm1, waits=[te, v_done] + swar)
                    def mm2(t, kt=kt, og=og, W=W, coff=coff, sbk=sbk):
                        return t.matmul(sbk.ap[:, 0:W],
                                        kaug2[:, kt * 128:(kt + 1) * 128],
                                        qa2_b[:, og * 512 + coff: og * 512 + coff + W],
                                        start=False, stop=True)
                    return P.emit("t", mm2, waits=[tnegm] + kar.pop(), inc="ts")

                def emit_px(item, idx, tmm2):
                    kt, kd, W, coff = _geo(item)
                    sbk = bank[STB[idx % 3]]
                    pp = idx % 3
                    def ex_(s, W=W, pp=pp, sbk=sbk):
                        return s.activation(pt_b[pp][:, 0:W], sbk.ap[:, 0:W], ACT.Exp)
                    tex = P.emit("s", ex_, waits=[tmm2] + pt_r[pp].take(), inc="ss")
                    sbk.free.append(tex)
                    if kd is None:
                        return tex
                    def msk(v, kd=kd, W=W, pp=pp):
                        return v.tensor_mul(pt_b[pp][:, 0:W], pt_b[pp][:, 0:W], mask_b[kd][:, 0:W])
                    tmsk = P.emit("v", msk, waits=[tex, t_masks[kd]], inc="vs")
                    P.emit("v", lambda v: v.drain())
                    return tmsk

                def emit_av(item, idx, av_wait, hl=hl, bkCT=bkCT, bkS=bkS):
                    kt, kd, W, coff = _geo(item)
                    pp = idx % 3
                    first = idx == 0
                    last = idx == nkt - 1
                    def av(t, hl=hl, kt=kt, W=W, coff=coff, pp=pp, first=first, last=last, bkCT=bkCT):
                        return t.matmul(bkCT.ap[0:DV, coff:coff + W],
                                        v_b[:, kt * 2 * DV + hl * DV: kt * 2 * DV + (hl + 1) * DV],
                                        pt_b[pp][:, 0:W], start=first, stop=last)
                    P.emit("t", av, waits=[av_wait] + (ct_war if first else []))
                    def sm(t, W=W, coff=coff, pp=pp, first=first, last=last, bkS=bkS):
                        return t.matmul(bkS.ap[0:1, coff:coff + W], ones_c[:],
                                        pt_b[pp][:, 0:W], start=first, stop=last)
                    tsm = P.emit("t", sm, waits=(s_war if first else []), inc="ts")
                    pt_r[pp].free.append(tsm)
                    return tsm

                kar = kak
                tmm2s = {0: emit_scores(kts[0], 0)}
                if nkt > 1:
                    tmm2s[1] = emit_scores(kts[1], 1)
                if pend_fin[0] is not None:
                    pend_fin[0]()
                    pend_fin[0] = None
                if pend_tail[0] is not None:
                    og1, te1, tail1 = pend_tail[0]
                    pend_tail[0] = None
                    toks[og1] = (te1, tail1())
                ct_war = bkCT.take()
                s_war = bkS.take()
                for idx, item in enumerate(kts):
                    avw = emit_px(item, idx, tmm2s.pop(idx))
                    if idx + 2 < nkt:
                        tmm2s[idx + 2] = emit_scores(kts[idx + 2], idx + 2)
                    tsm = emit_av(item, idx, avw)
                ct_done = tsm
                def rcp(v, bkS=bkS):
                    return v.reciprocal(rs_row[0:1, 0:512], bkS.ap[0:1, 0:512])
                trc = P.emit("v", rcp, waits=[ct_done] + rs_r.take(), inc="vs")
                bkS.free.append(trc)

                def fin(bkCT=bkCT, ct_done=ct_done, og=og, h=h, trc=trc):
                    trep = plan_rep(rs_row[0:1, 0:512], rep_ct, repct_r, 128, [trc])
                    rs_r.free.append(P.tok("ts"))
                    def ctxev(v, bkCT=bkCT):
                        return v.scalar_tensor_tensor(ctx_st[0:DV, 0:512], bkCT.ap[0:DV, 0:512], 1.0,
                                                      rep_ct[0:DV, 0:512], ALU.mult, ALU.mult)
                    tcv = P.emit("v", ctxev, waits=[ct_done, trep] + ctxst_r.take(), inc="vs")
                    bkCT.free.append(tcv)
                    repct_r.free.append(tcv)
                    tdma = Q.dma(lambda q, h=h, og=og: q.dma_start(
                        ctx_s[h * DV:(h + 1) * DV, og * 512:(og + 1) * 512], ctx_st[0:DV, 0:512]),
                        waits=[tcv])
                    ctxst_r.free.append(tdma)

                pend_fin[0] = fin
                qn_r.free.append(P.tok("ts"))
                qa2_r.free.append(P.tok("ts"))

            te0, tail0 = emit_prep(0)
            toks = {0: (te0, tail0())}
            for og in range(NBLK):
                if og + 1 < NBLK:
                    te1, tail1 = emit_prep(og + 1)
                    pend_tail[0] = (og + 1, te1, tail1)
                emit_attn(og, *toks[og])
                if pend_tail[0] is not None:
                    og1, te1, tail1 = pend_tail[0]
                    pend_tail[0] = None
                    toks[og1] = (te1, tail1())
            kn_r[hl].free.append(P.tok("ts"))
            kn2_r[hl].free.append(tlnm)
            lnm2_r.free.append(P.tok("vs"))
        v_r.free.append(P.tok("ts"))
        wqb_r[pbi].free.append(P.tok("ts"))
        wkvb_r[pbi].free.append(P.tok("ts"))

    if pend_fin[0] is not None:
        pend_fin[0]()
        pend_fin[0] = None
    ctx_all_done = Q.all_done()

    barrier()
    stB.close()
    stAB.close()
    stB2 = ExitStack()
    sb2 = lambda n, s, d: stB2.enter_context(nc.sbuf_tensor(n, s, d))
    wo_b = sb2("wo_b", [128, DC * HID], BF16)
    ctxg = sb2("ctxg", [128, (H * DV // 128) * 512], BF16)
    ctxg_r = Res()
    xog = sb2("xog", [128, DC * 512], F32)
    xog_r = Res()
    y1g = sb2("y1g", [128, DC * 512], BF16)
    y1g_r = Res()
    y1st = sb2("y1st", [128, 512], F32)
    y1st_r = Res()
    sqs = sb2("sqs", [128, 512], BF16)
    sqs_r = Res()
    var_row = sb2("var_row", [1, 512], F32)
    mu2_row = sb2("mu2_row", [1, 512], F32)
    rowsb_r = Res()
    mu_row = sb2("mu_row", [1, 512], F32)
    a2_row = sb2("a2_row", [1, 512], F32)
    a2rep = sb2("a2rep", [128, 512], F32)
    a2rep_r = Res()
    murep = sb2("murep", [128, 512], F32)
    murep_r = Res()
    nrm_t = sb2("nrm_t", [128, 512], F32)
    nrm_r = Res()
    y1n_st = sb2("y1n_st", [128, 512], BF16)
    y1n_r = Res()

    two = G.dma(lambda g: g.dma_start(wo_b[:], woT.rearrange("(c p) q -> p c q", p=128)))
    two = G.dma(lambda g: g.dma_start(wo_b[:], woT_lo.rearrange("(c p) q -> p c q", p=128),
                                      accum_op=ALU.add), waits=[two])
    HC = H * DV // 128
    for tg in range(OWN // 512):
        tctx = SQ.dma(lambda q, tg=tg: q.dma_start(
            ctxg[:], ctx_s.rearrange("(c p) t -> p c t", p=128)[:, :, tg * 512:(tg + 1) * 512]),
            waits=[x for x in ctx_all_done] + ctxg_r.take())
        ctx_all_done = []
        txo = SQ.dma(lambda q, tg=tg: q.dma_start(
            xog[:], xof.rearrange("(c p) t -> p c t", p=128)[:, :, tg * 512:(tg + 1) * 512]),
            waits=xog_r.take())
        y1war = y1g_r.take()
        bkMu = bank[4]
        bkSq = bank[5]
        mu_war = bkMu.take()
        sq_war = bkSq.take()
        for dc in range(DC):
            bk = bank[dc % 4]
            war = bk.take()
            for hc in range(HC):
                def mm(t, dc=dc, hc=hc, bk=bk):
                    return t.matmul(bk.ap[:, :],
                                    wo_b[:, hc * HID + dc * 128: hc * HID + (dc + 1) * 128],
                                    ctxg[:, hc * 512:(hc + 1) * 512],
                                    start=(hc == 0), stop=(hc == HC - 1))
                tk = P.emit("t", mm, waits=([two, tctx] + war) if hc == 0 else [],
                            inc="ts" if hc == HC - 1 else None)
            def evf(v, dc=dc, bk=bk):
                return v.tensor_add(y1st[:], bk.ap[:, :], xog[:, dc * 512:(dc + 1) * 512])
            tef = P.emit("v", evf, waits=[tk, txo] + y1st_r.take(), inc="vs")
            bk.free.append(tef)
            td1 = Q.dma(lambda q, tg=tg, dc=dc: q.dma_start(
                y1_s[dc * 128:(dc + 1) * 128, tg * 512:(tg + 1) * 512], y1st[:]), waits=[tef])
            def evb(v, dc=dc):
                return v.tensor_copy(y1g[:, dc * 512:(dc + 1) * 512], y1st[:])
            teb = P.emit("v", evb, waits=[tef] + (y1war if dc == 0 else []), inc="vs")
            y1st_r.free.append(td1)
            y1st_r.free.append(teb)
            def mmu(t, dc=dc, bkMu=bkMu):
                return t.matmul(bkMu.ap[0:1, 0:512], ones_c[:], y1g[:, dc * 512:(dc + 1) * 512],
                                start=(dc == 0), stop=(dc == DC - 1))
            tmu = P.emit("t", mmu, waits=[teb] + (mu_war if dc == 0 else []), inc="ts")
            def sq_(v, dc=dc):
                sl = y1g[:, dc * 512:(dc + 1) * 512]
                return v.tensor_mul(sqs[:], sl, sl)
            tsq = P.emit("v", sq_, waits=[teb] + sqs_r.take(), inc="vs")
            def msq(t, dc=dc, bkSq=bkSq):
                return t.matmul(bkSq.ap[0:1, 0:512], ones_c[:], sqs[:],
                                start=(dc == 0), stop=(dc == DC - 1))
            tms = P.emit("t", msq, waits=[tsq] + (sq_war if dc == 0 else []), inc="ts")
            sqs_r.free.append(tms)
        rb_war = rowsb_r.take()
        def r0(v, bkMu=bkMu):
            return v.tensor_scalar_mul(mu_row[0:1, :], bkMu.ap[0:1, 0:512], 1.0 / HID)
        t0 = P.emit("v", r0, waits=[tmu] + rb_war, inc="vs")
        bkMu.free.append(t0)
        def r1(v, bkSq=bkSq):
            return v.tensor_scalar_mul(var_row[0:1, :], bkSq.ap[0:1, 0:512], 1.0 / HID)
        t1_ = P.emit("v", r1, waits=[tms], inc="vs")
        bkSq.free.append(t1_)
        P.emit("v", lambda v: v.drain())
        def r2(v):
            return v.tensor_mul(mu2_row[0:1, :], mu_row[0:1, :], mu_row[0:1, :])
        t2_ = P.emit("v", r2, waits=[t0], inc="vs")
        P.emit("v", lambda v: v.drain())
        def r3(v):
            return v.tensor_sub(var_row[0:1, :], var_row[0:1, :], mu2_row[0:1, :])
        t3_ = P.emit("v", r3, waits=[t1_, t2_], inc="vs")
        P.emit("v", lambda v: v.drain())
        def lnv(s):
            return s.activation(a2_row[0:1, :], var_row[0:1, :], ACT.Ln, bias=eps5[0:1, 0:1])
        t4 = P.emit("s", lnv, waits=[t3_, t_eps5], inc="ss")
        P.emit("s", lambda s: s.drain())
        def exv(s):
            return s.activation(a2_row[0:1, :], a2_row[0:1, :], ACT.Exp, scale=-0.5)
        t5 = P.emit("s", exv, waits=[t4], inc="ss")
        ta2 = plan_rep(a2_row[0:1, 0:512], a2rep, a2rep_r, 128, [t5])
        tmu_ = plan_rep(mu_row[0:1, 0:512], murep, murep_r, 128, [t0])
        rowsb_r.free.append(P.tok("ts"))
        for dc in range(DC):
            def n1(v, dc=dc):
                return v.tensor_sub(nrm_t[:], y1g[:, dc * 512:(dc + 1) * 512], murep[:])
            tn1 = P.emit("v", n1, waits=[ta2, tmu_] + nrm_r.take(), inc="vs")
            def n2(v):
                return v.tensor_mul(y1n_st[:], nrm_t[:], a2rep[:])
            tn2 = P.emit("v", n2, waits=[tn1] + y1n_r.take(), inc="vs")
            nrm_r.free.append(tn2)
            tnd = Q.dma(lambda q, tg=tg, dc=dc: q.dma_start(
                y1n_s[dc * 128:(dc + 1) * 128, tg * 512:(tg + 1) * 512], y1n_st[:]), waits=[tn2])
            y1n_r.free.append(tnd)
            y1g_r.free.append(tn2)
        a2rep_r.free.append(P.tok("vs"))
        murep_r.free.append(P.tok("vs"))
        y1g_r.free.append(P.tok("ts"))
        ctxg_r.free.append(P.tok("ts"))
        xog_r.free.append(P.tok("vs"))

    y1_done = Q.all_done()

    barrier()
    stB2.close()
    stC = ExitStack()
    Hbuf = stC.enter_context(nc.sbuf_tensor("Hbuf", [128, FC * GRP], BF16))
    Hbuf_r = Res()

    DQW = 256 if HID >= 2048 else 128
    NDQ = HID // DQW
    for fg in range(NFG):
        if fg:
            barrier()
        st1 = ExitStack()
        s1 = lambda n, s, d: st1.enter_context(nc.sbuf_tensor(n + str(fg), s, d))
        lnf = s1("lnf", [128, DC * GRP], BF16)
        w1s = [s1(f"w1s{i}", [128, DC * 128], BF16) for i in range(2)]
        w1s_r = [Res(), Res()]
        w3s = [s1(f"w3s{i}", [128, DC * 128], BF16) for i in range(2)]
        w3s_r = [Res(), Res()]
        sil = s1("sil", [128, 512], BF16)
        sil_r = Res()

        tln = G.dma(lambda g, fg=fg, lnf=lnf: g.dma_start(
            lnf[:], y1n_s.rearrange("(c p) t -> p c t", p=128)[:, :, fg * GRP:(fg + 1) * GRP]),
            waits=(y1_done if fg == 0 else []))
        hbuf_war = Hbuf_r.take()
        hm_toks = {}
        h1_done = []

        def emit_h1(j):
            for nn in range(GN):
                bk = bank[4 + (j + nn) % 2]
                war = bk.take()
                def mm(t, j=j, nn=nn, bk=bk):
                    return t.matmul(bk.ap[:, :], m64_b[:],
                                    Hbuf[:, j * GRP + nn * 512: j * GRP + nn * 512 + 512],
                                    start=True, stop=True)
                tk = P.emit("t", mm, waits=[hm_toks[j], t_m64] + war, inc="ts")
                def ev(v, j=j, nn=nn, bk=bk):
                    return v.tensor_copy(Hbuf[:, j * GRP + nn * 512: j * GRP + nn * 512 + 512],
                                         bk.ap[:, :])
                te = P.emit("v", ev, waits=[tk], inc="vs")
                bk.free.append(te)
                h1_done.append(te)

        for fc in range(FC):
            wbi = fc % 2
            tw1 = G.dma(lambda g, fc=fc, wbi=wbi, w1s=w1s: g.dma_start(
                w1s[wbi][:], w1pt.rearrange("(c p) f -> p c f", p=128)[:, :, fc * 128:(fc + 1) * 128]),
                waits=w1s_r[wbi].take())
            tw1 = G.dma(lambda g, fc=fc, wbi=wbi, w1s=w1s: g.dma_start(
                w1s[wbi][:],
                w1pt_lo.rearrange("(c p) f -> p c f", p=128)[:, :, fc * 128:(fc + 1) * 128],
                accum_op=ALU.add), waits=[tw1])
            tw3 = G.dma(lambda g, fc=fc, wbi=wbi, w3s=w3s: g.dma_start(
                w3s[wbi][:], w3pt.rearrange("(c p) f -> p c f", p=128)[:, :, fc * 128:(fc + 1) * 128]),
                waits=w3s_r[wbi].take())
            tw3 = G.dma(lambda g, fc=fc, wbi=wbi, w3s=w3s: g.dma_start(
                w3s[wbi][:],
                w3pt_lo.rearrange("(c p) f -> p c f", p=128)[:, :, fc * 128:(fc + 1) * 128],
                accum_op=ALU.add), waits=[tw3])
            for nn in range(GN):
                bkG = bank[(2 * (fc % 2)) % 4]
                bkL = bank[(2 * (fc % 2) + 1) % 4]
                warG = bkG.take()
                warL = bkL.take()
                for dc in range(DC):
                    def mmg(t, fc=fc, nn=nn, dc=dc, wbi=wbi, bkG=bkG, w1s=w1s, lnf=lnf):
                        return t.matmul(bkG.ap[:, :],
                                        w1s[wbi][:, dc * 128:(dc + 1) * 128],
                                        lnf[:, dc * GRP + nn * 512: dc * GRP + nn * 512 + 512],
                                        start=(dc == 0), stop=(dc == DC - 1))
                    tg_ = P.emit("t", mmg, waits=([tw1, tln] + warG) if dc == 0 else [],
                                 inc="ts" if dc == DC - 1 else None)
                for dc in range(DC):
                    def mml(t, fc=fc, nn=nn, dc=dc, wbi=wbi, bkL=bkL, w3s=w3s, lnf=lnf):
                        return t.matmul(bkL.ap[:, :],
                                        w3s[wbi][:, dc * 128:(dc + 1) * 128],
                                        lnf[:, dc * GRP + nn * 512: dc * GRP + nn * 512 + 512],
                                        start=(dc == 0), stop=(dc == DC - 1))
                    tl_ = P.emit("t", mml, waits=([tw3] + warL) if dc == 0 else [],
                                 inc="ts" if dc == DC - 1 else None)
                def sl_(s, bkG=bkG, sil=sil):
                    return s.activation(sil[:], bkG.ap[:, :], ACT.Silu)
                tsl = P.emit("s", sl_, waits=[tg_] + sil_r.take(), inc="ss")
                bkG.free.append(tsl)
                def hm(v, fc=fc, nn=nn, bkL=bkL, sil=sil):
                    return v.scalar_tensor_tensor(Hbuf[:, fc * GRP + nn * 512: fc * GRP + nn * 512 + 512],
                                                  bkL.ap[:, :], 1.0, sil[:], ALU.mult, ALU.mult)
                thm = P.emit("v", hm, waits=[tl_, tsl] + (hbuf_war if (fc == 0 and nn == 0) else []), inc="vs")
                bkL.free.append(thm)
                sil_r.free.append(thm)
                hm_toks[fc] = thm
            w1s_r[wbi].free.append(P.tok("ts"))
            w3s_r[wbi].free.append(P.tok("ts"))
            if fc >= 2:
                emit_h1(fc - 2)
        emit_h1(FC - 2)
        emit_h1(FC - 1)
        st1.close()
        stage_toks = [P.tok("ts"), P.tok("vs"), P.tok("ss")]

        st2 = ExitStack()
        s2 = lambda n, s, d: st2.enter_context(nc.sbuf_tensor(n + str(fg), s, d))
        w2b = [s2(f"w2b{i}", [128, FC * DQW], BF16) for i in range(2)]
        w2b_r = [Res(), Res()]
        y1r = [s2(f"y1r{i}", [128, 512], F32) for i in range(2)]
        y1r_r = [Res(), Res()]
        outst = [s2(f"outst{i}", [128, 512], F32) for i in range(2)]
        outst_r = [Res(), Res()]
        for dq in range(NDQ):
            wbi = dq % 2
            ld_toks = []
            war = w2b_r[wbi].take()
            def ld(q, dq=dq, wbi=wbi, w2b=w2b):
                v = w2f_s.rearrange("(j q) d -> q j d", q=128)
                return q.dma_start(
                    w2b[wbi][:].rearrange("q (j d) -> q j d", d=DQW),
                    v[:, :, dq * DQW:(dq + 1) * DQW])
            tk = SQ.dma(ld, waits=war + w2f_done + stage_toks)
            ld_toks.append(tk)
            for nn in range(GN):
                for mc in range(DQW // 128):
                    bk = bank[(nn * 2 + mc) % 4]
                    warb = bk.take()
                    ty1 = SQ.dma(lambda q, fg=fg, dq=dq, nn=nn, mc=mc, wbi=wbi, y1r=y1r: q.dma_start(
                        y1r[wbi][:], y1_s[dq * DQW + mc * 128: dq * DQW + (mc + 1) * 128,
                                          fg * GRP + nn * 512: fg * GRP + nn * 512 + 512]),
                        waits=y1r_r[wbi].take() + (y1_done if (fg == 0 and dq == 0 and nn == 0 and mc == 0) else []))
                    for j in range(FC):
                        def mm(t, dq=dq, nn=nn, mc=mc, j=j, wbi=wbi, bk=bk, w2b=w2b):
                            return t.matmul(bk.ap[:, :],
                                            w2b[wbi][:, j * DQW + mc * 128: j * DQW + (mc + 1) * 128],
                                            Hbuf[:, j * GRP + nn * 512: j * GRP + nn * 512 + 512],
                                            start=(j == 0), stop=(j == FC - 1))
                        tk = P.emit("t", mm,
                                    waits=(ld_toks + h1_done + warb) if j == 0 else [],
                                    inc="ts" if j == FC - 1 else None)
                    h1_done = []
                    def ev(v, wbi=wbi, bk=bk, y1r=y1r, outst=outst):
                        return v.tensor_add(outst[wbi][:], bk.ap[:, :], y1r[wbi][:])
                    te = P.emit("v", ev, waits=[tk, ty1] + outst_r[wbi].take(), inc="vs")
                    bk.free.append(te)
                    y1r_r[wbi].free.append(te)
                    tod = Q.dma(lambda q, fg=fg, dq=dq, nn=nn, mc=mc, wbi=wbi, outst=outst: q.dma_start(
                        out[dq * DQW + mc * 128: dq * DQW + (mc + 1) * 128,
                            fg * GRP + nn * 512: fg * GRP + nn * 512 + 512], outst[wbi][:]),
                        waits=[te])
                    outst_r[wbi].free.append(tod)
            w2b_r[wbi].free.append(P.tok("ts"))
        Hbuf_r.free.append(P.tok("ts"))
        st2.close()

    def fin(q):
        return None
    final_waits = Q.all_done() + G.all_done() + SQ.all_done()
    P.emit("q", None, waits=final_waits)

    with nc.Block() as block:
        def runner(eng, name):
            last_wait = {}
            for fn, waits, inc, amt in P.ops[name]:
                for s, v in waits:
                    if last_wait.get(s, -1) < v:
                        eng.wait_ge(sems[s], v)
                        last_wait[s] = v
                ins = fn(eng) if fn is not None else None
                if inc is not None:
                    assert ins is not None, f"inc {inc} needs an instruction on {name}"
                    ins.then_inc(sems[inc], amt)

        @block.sync
        def _(q):
            runner(q, "q")

        @block.gpsimd
        def _(g):
            runner(g, "g")

        @block.tensor
        def _(t):
            runner(t, "t")

        @block.vector
        def _(v):
            runner(v, "v")

        @block.scalar
        def _(s):
            runner(s, "s")

    stC.close()
    gst.close()
    return nc


def _prep_inputs(cfg, x_full, w, b, r):
    own = own_tokens(cfg, r)
    xb = x_full[b]
    c = host_consts(cfg, r)
    perm = w1_perm(cfg["FFNH"])
    d = {
        "xf": np.ascontiguousarray(xb.T),
        "xt": np.ascontiguousarray(xb),
        "xof": np.ascontiguousarray(xb[own].T),
        "xot": np.ascontiguousarray(xb[own]),
        "wqaT": w["wqaT"], "wkvaT": w["wkvaT"], "wqbT": w["wqbT"],
        "wkvbT": w["wkvbT"], "woT": w["woT"], "w1pt": w["w1pt"],
        "w3pt": w["w3pt"], "w2t": w["w2t"], "gq": w["gq"], "gkv": w["gkv"],
        "wqaT_lo": w["wqaT_lo"], "wkvaT_lo": w["wkvaT_lo"], "wqbT_lo": w["wqbT_lo"],
        "wkvbT_lo": w["wkvbT_lo"], "woT_lo": w["woT_lo"], "w1pt_lo": w["w1pt_lo"],
        "w3pt_lo": w["w3pt_lo"], "w2t_lo": w["w2t_lo"],
        "h128": c["h128"], "m64": c["m64"], "r64t": c["r64t"], "idenf": c["idenf"],
        "cosf": c["cosf"], "sinf": c["sinf"], "cosfo": c["cosfo"], "sinfo": c["sinfo"],
        "cosf_lo": c["cosf_lo"], "sinf_lo": c["sinf_lo"],
        "cosfo_lo": c["cosfo_lo"], "sinfo_lo": c["sinfo_lo"],
    }
    for kd in range(8):
        d[f"mask{kd}"] = c[f"mask{kd}"]
    return d, own


def _hilo(a):
    a64 = np.asarray(a, np.float64)
    hi = a64.astype(np.float32)
    lo = (a64 - hi).astype(np.float32)
    return hi, lo


def _shared_weights(cfg, L0_attn_w_qa, L0_attn_q_a_ln_w, L0_attn_w_qb, L0_attn_w_kva,
                    L0_attn_kv_a_ln_w, L0_attn_w_kvb, L0_attn_w_o, L0_ffn_W1,
                    L0_ffn_W2, L0_ffn_W3):
    QLR, KVLR, FFNH = cfg["QLR"], cfg["KVLR"], cfg["FFNH"]
    QC, KC = QLR // 128, KVLR // 128
    perm = w1_perm(FFNH)
    w = {}
    for name, val, tf in (
        ("wqaT", L0_attn_w_qa, lambda a: a.T),
        ("wkvaT", L0_attn_w_kva, lambda a: a.T),
        ("wqbT", L0_attn_w_qb, lambda a: a.T),
        ("wkvbT", L0_attn_w_kvb, lambda a: a.T),
        ("woT", L0_attn_w_o, lambda a: a.T),
        ("w1pt", L0_ffn_W1, lambda a: a.T[:, perm]),
        ("w3pt", L0_ffn_W3, lambda a: a.T[:, perm]),
        ("w2t", L0_ffn_W2, lambda a: a.T),
    ):
        hi, lo = _hilo(np.asarray(val))
        w[name] = np.ascontiguousarray(tf(hi))
        w[name + "_lo"] = np.ascontiguousarray(tf(lo))
    w["gq"] = np.ascontiguousarray(
        np.asarray(L0_attn_q_a_ln_w, np.float32).reshape(QC, 128).T)
    w["gkv"] = np.ascontiguousarray(
        np.asarray(L0_attn_kv_a_ln_w, np.float32).reshape(KC, 128).T)
    return w


def kernel(L0_attn_hidden_states, L0_attn_w_qa, L0_attn_q_a_ln_w, L0_attn_w_qb,
           L0_attn_w_kva, L0_attn_kv_a_ln_w, L0_attn_w_kvb, L0_attn_w_o,
           L0_ffn_W1, L0_ffn_W2, L0_ffn_W3):
    cfg = REAL_CFG
    x = np.ascontiguousarray(np.asarray(L0_attn_hidden_states, dtype=np.float32))
    B = x.shape[0]
    w = _shared_weights(cfg, L0_attn_w_qa, L0_attn_q_a_ln_w, L0_attn_w_qb,
                        L0_attn_w_kva, L0_attn_kv_a_ln_w, L0_attn_w_kvb,
                        L0_attn_w_o, L0_ffn_W1, L0_ffn_W2, L0_ffn_W3)
    nc = build_nc(cfg)
    in_maps = []
    owns = []
    for c in range(8):
        b, r = c // 2, c % 2
        d, own = _prep_inputs(cfg, x, w, b, r)
        in_maps.append(d)
        owns.append((b, own))
    res = run_bass_kernel_spmd(nc, in_maps, core_ids=list(range(8)))
    outp = np.empty_like(x)
    for c in range(8):
        b, own = owns[c]
        outp[b, own, :] = np.asarray(res.results[c]["out"], np.float32).T
    return outp


# version 15 (final)
# speedup vs baseline: 1.3353x; 1.3353x over previous
import math
from collections import defaultdict
from contextlib import ExitStack

import numpy as np

import concourse.bass as bass
import concourse.mybir as mybir
from concourse.bass_utils import run_bass_kernel_spmd

F32 = mybir.dt.float32
BF16 = mybir.dt.bfloat16
ALU = mybir.AluOpType
ACT = mybir.ActivationFunctionType
AX = mybir.AxisListType

DR, DN, DV = 64, 128, 128
DQ = DN + DR
S4 = float(DQ) ** (-0.25)

REAL_CFG = dict(HID=2048, S=4096, QLR=1536, KVLR=512, H=16, FFNH=8192, THETA=10000.0)
DEV_CFG = dict(HID=512, S=2048, QLR=384, KVLR=256, H=4, FFNH=2048, THETA=10000.0,
               LO=True)


def hadamard(n):
    h = np.array([[1.0]], np.float32)
    while h.shape[0] < n:
        h = np.block([[h, h], [h, -h]]).astype(np.float32)
    return h


def own_tokens(cfg, r):
    NOWNT = (cfg["S"] // 128) // 2
    own = (np.arange(NOWNT)[:, None] * 2 + r) * 128 + np.arange(128)[None, :]
    return own.reshape(-1)


def host_consts(cfg, r):
    S, FFNH = cfg["S"], cfg["FFNH"]
    CW = FFNH // 128
    own = own_tokens(cfg, r)
    c = {}
    c["h128"] = hadamard(128)
    c["m64"] = np.kron(np.eye(128 // CW, dtype=np.float32), hadamard(CW)).astype(np.float32)
    R = np.zeros((DR, DR), np.float32)
    for i in range(DR // 2):
        R[i, i + DR // 2] = -1.0
        R[i + DR // 2, i] = 1.0
    c["r64t"] = np.ascontiguousarray(R.T)
    c["idenf"] = np.eye(128, dtype=np.float32)
    inv = 1.0 / cfg["THETA"] ** (np.arange(0, DR, 2, dtype=np.float64) / DR)
    fr = np.outer(np.arange(S, dtype=np.float64), inv)
    cos64 = np.concatenate([np.cos(fr), np.cos(fr)], 1).T * np.float64(S4)
    sin64 = np.concatenate([np.sin(fr), np.sin(fr)], 1).T * np.float64(S4)
    cos = cos64.astype(np.float32)
    sin = sin64.astype(np.float32)
    cos_lo = (cos64 - cos).astype(np.float32)
    sin_lo = (sin64 - sin).astype(np.float32)
    c["cosf"], c["sinf"] = cos.copy(), sin.copy()
    c["cosf_lo"], c["sinf_lo"] = cos_lo.copy(), sin_lo.copy()
    c["cosfo"] = np.ascontiguousarray(cos[:, own])
    c["sinfo"] = np.ascontiguousarray(sin[:, own])
    c["cosfo_lo"] = np.ascontiguousarray(cos_lo[:, own])
    c["sinfo_lo"] = np.ascontiguousarray(sin_lo[:, own])
    for kd in range(8):
        W = 512 - 128 * (kd // 2)
        coff = 128 * (kd // 2)
        i = np.arange(128)[:, None]
        jj = np.arange(W)[None, :]
        col = coff + jj
        ct = col // 128
        ci = col % 128
        qt = 2 * ct + r
        m = np.where(qt > kd, 1.0, np.where(qt == kd, (ci >= i).astype(np.float32), 0.0))
        c[f"mask{kd}"] = m.astype(np.float32)
    return c


def w1_perm(FFNH):
    CW = FFNH // 128
    i = np.arange(FFNH)
    return (i % CW) * 128 + i // CW


ENGS = ("q", "g", "t", "v", "s")


class Prog:
    def __init__(self):
        self.ops = {e: [] for e in ENGS}
        self.cnt = defaultdict(int)

    def emit(self, eng, fn=None, waits=(), inc=None, amt=1):
        ws = []
        for w in waits:
            if w is None:
                continue
            s, v = w
            if v > 0:
                ws.append((s, v))
        self.ops[eng].append((fn, ws, inc, amt))
        if inc:
            self.cnt[inc] += amt
            return (inc, self.cnt[inc])
        return None

    def tok(self, sem):
        return (sem, self.cnt[sem])


class DmaQ:
    def __init__(self, prog, eng, base, n=8):
        self.prog, self.eng = prog, eng
        self.names = [f"{base}{i}" for i in range(n)]
        self.rr = 0

    def dma(self, fn, waits=()):
        name = self.names[self.rr]
        self.rr = (self.rr + 1) % len(self.names)
        ws = list(waits)
        prev = self.prog.cnt[name]
        if prev:
            ws.append((name, prev))
        return self.prog.emit(self.eng, fn, ws, inc=name, amt=16)

    def all_done(self):
        return [(n, self.prog.cnt[n]) for n in self.names if self.prog.cnt[n]]


class _KList:

    def __init__(self, toks):
        self.toks = list(toks)

    def pop(self):
        t, self.toks = self.toks, []
        return t


class Res:

    def __init__(self, ap=None):
        self.ap = ap
        self.free = []

    def take(self):
        f, self.free = self.free, []
        return f


def build_nc(cfg):
    HID, S, QLR, KVLR, H, FFNH = (cfg[k] for k in ("HID", "S", "QLR", "KVLR", "H", "FFNH"))
    DC = HID // 128
    QC = QLR // 128
    KC = KVLR // 128
    SEQT = S // 128
    NG = SEQT // 4
    NOWNT = SEQT // 2
    OWN = NOWNT * 128
    NBLK = NOWNT // 4
    NPAIR = H // 2
    FC = FFNH // 128
    CW = FFNH // 128
    GRP = 1024 if OWN % 1024 == 0 else 512
    NFG = OWN // GRP
    GN = GRP // 512
    KVW = KVLR + DR
    KVMC = [128] * KC + [DR]
    KVWID = 2 * (DN + DV)
    PHI = 128 // CW

    nc = bass.Bass("TRN2", target_bir_lowering=False)
    dram = lambda n, s, d, k: nc.dram_tensor(n, s, d, kind=k).ap()
    I, O, N = "ExternalInput", "ExternalOutput", "Internal"

    xf = dram("xf", [HID, S], F32, I)
    xt = dram("xt", [S, HID], F32, I)
    xof = dram("xof", [HID, OWN], F32, I)
    xot = dram("xot", [OWN, HID], F32, I)
    wqaT = dram("wqaT", [HID, QLR], F32, I)
    wkvaT = dram("wkvaT", [HID, KVW], F32, I)
    wqbT = dram("wqbT", [QLR, H * DQ], F32, I)
    wkvbT = dram("wkvbT", [KVLR, H * (DN + DV)], F32, I)
    woT = dram("woT", [HID, HID], F32, I)
    w1pt = dram("w1pt", [HID, FFNH], F32, I)
    w3pt = dram("w3pt", [HID, FFNH], F32, I)
    w2t = dram("w2t", [FFNH, HID], F32, I)
    gq_i = dram("gq", [128, QC], F32, I)
    gkv_i = dram("gkv", [128, KC], F32, I)
    h128_i = dram("h128", [128, 128], F32, I)
    m64_i = dram("m64", [128, 128], F32, I)
    r64t_i = dram("r64t", [DR, DR], F32, I)
    idenf_i = dram("idenf", [128, 128], F32, I)
    cosf_i = dram("cosf", [DR, S], F32, I)
    sinf_i = dram("sinf", [DR, S], F32, I)
    cosfo_i = dram("cosfo", [DR, OWN], F32, I)
    sinfo_i = dram("sinfo", [DR, OWN], F32, I)
    LO = bool(cfg.get("LO"))
    if LO:
        wqaT_lo = dram("wqaT_lo", [HID, QLR], F32, I)
        wkvaT_lo = dram("wkvaT_lo", [HID, KVW], F32, I)
        wqbT_lo = dram("wqbT_lo", [QLR, H * DQ], F32, I)
        wkvbT_lo = dram("wkvbT_lo", [KVLR, H * (DN + DV)], F32, I)
        woT_lo = dram("woT_lo", [HID, HID], F32, I)
        w1pt_lo = dram("w1pt_lo", [HID, FFNH], F32, I)
        w3pt_lo = dram("w3pt_lo", [HID, FFNH], F32, I)
        w2t_lo = dram("w2t_lo", [FFNH, HID], F32, I)
        cosf_lo = dram("cosf_lo", [DR, S], F32, I)
        sinf_lo = dram("sinf_lo", [DR, S], F32, I)
        cosfo_lo = dram("cosfo_lo", [DR, OWN], F32, I)
        sinfo_lo = dram("sinfo_lo", [DR, OWN], F32, I)
    mask_i = [dram(f"mask{kd}", [128, 512 - 128 * (kd // 2)], F32, I) for kd in range(8)]
    out = dram("out", [HID, OWN], F32, O)
    ctx_s = dram("ctx_s", [H * DV, OWN], BF16, N)
    y1_s = dram("y1_s", [HID, OWN], F32, N)
    y1n_s = dram("y1n_s", [HID, OWN], BF16, N)
    w2f_s = dram("w2f_s", [FFNH, HID], BF16, N)

    gst = ExitStack()
    sbg = lambda n, s, d: gst.enter_context(nc.sbuf_tensor(n, s, d))
    pb = [gst.enter_context(nc.psum_tensor(f"pb{i}", [128, 512], F32)) for i in range(8)]
    bank = [Res(pb[i]) for i in range(8)]
    semnames = ([f"qd{i}" for i in range(8)] + [f"gd{i}" for i in range(8)]
                + [f"sd{i}" for i in range(8)] + ["ts", "vs", "ss", "gs"])
    sems = {n: gst.enter_context(nc.semaphore(n)) for n in semnames}

    P = Prog()
    Q = DmaQ(P, "q", "qd")
    G = DmaQ(P, "g", "gd")
    SQ = DmaQ(P, "s", "sd")

    def barrier():
        toks = ([("ts", P.cnt["ts"]), ("vs", P.cnt["vs"]), ("ss", P.cnt["ss"])]
                + Q.all_done() + G.all_done() + SQ.all_done())
        for e in ENGS:
            P.emit(e, None, waits=toks)

    ones_c = sbg("ones_c", [128, 1], BF16)
    onesr_f = sbg("onesr_f", [1, 128], F32)
    h128_b = sbg("h128_b", [128, 128], BF16)
    m64_b = sbg("m64_b", [128, 128], BF16)
    r64t_b = sbg("r64t_b", [DR, DR], BF16)
    idenf = sbg("idenf_s", [128, 128], F32)
    gq = sbg("gq_s", [128, QC], F32)
    gkv = sbg("gkv_s", [128, KC], F32)

    eps5 = sbg("eps5", [1, 1], F32)
    eps6 = sbg("eps6", [1, 1], F32)
    t_ones = P.emit("v", lambda v: v.memset(ones_c[:], 1.0), inc="vs")
    t_eps5 = P.emit("v", lambda v: v.memset(eps5[:], 1e-5), inc="vs")
    t_eps6 = P.emit("v", lambda v: v.memset(eps6[:], 1e-6), inc="vs")
    t_onesr = P.emit("v", lambda v: v.memset(onesr_f[:], 1.0), inc="vs")
    t_h128 = G.dma(lambda g: g.dma_start(h128_b[:], h128_i))
    t_m64 = G.dma(lambda g: g.dma_start(m64_b[:], m64_i))
    t_r64 = G.dma(lambda g: g.dma_start(r64t_b[:], r64t_i))
    t_idenf = Q.dma(lambda q: q.dma_start(idenf[:], idenf_i))
    t_gq = Q.dma(lambda q: q.dma_start(gq[:], gq_i))
    t_gkv = Q.dma(lambda q: q.dma_start(gkv[:], gkv_i))

    stAB = ExitStack()
    sab = lambda n, s, d: stAB.enter_context(nc.sbuf_tensor(n, s, d))
    ckv_fm = sab("ckv_fm", [128, KC * S], BF16)
    kaug2 = sab("kaug2", [DR + 1, S], BF16)
    cq_fm = sab("cq_fm", [128, QC * OWN], BF16)
    dbg_u_sb = sab("dbg_u_sb", [128, KC * S], BF16) if cfg.get("DBG") else None
    t_kones = P.emit("v", lambda v: v.memset(kaug2[DR:DR + 1, :], 1.0), inc="vs")

    stW2 = ExitStack()
    w2in = [stW2.enter_context(nc.sbuf_tensor(f"w2in{i}", [128, HID], BF16)) for i in range(2)]
    w2st = [stW2.enter_context(nc.sbuf_tensor(f"w2st{i}", [128, HID], BF16)) for i in range(2)]
    w2in_r = [Res(), Res()]
    w2st_r = [Res(), Res()]
    stStats = ExitStack()
    stat_muf = [stStats.enter_context(nc.sbuf_tensor(f"stat_muf{i}", [1, 512], F32)) for i in range(2)]
    stat_varf = [stStats.enter_context(nc.sbuf_tensor(f"stat_varf{i}", [1, 512], F32)) for i in range(2)]
    stat_corr = [stStats.enter_context(nc.sbuf_tensor(f"stat_corr{i}", [1, 512], BF16)) for i in range(2)]
    stat_mv_r = [Res(), Res()]
    w2scale = 1.0 / math.sqrt(FFNH)
    NH = HID // 512
    _fold_c = [0]

    def fold_step(nsteps):
        for _ in range(nsteps):
            c = _fold_c[0]
            if c >= FC:
                return
            _fold_c[0] = c + 1
            bi = c % 2
            tin = G.dma(lambda g, c=c, bi=bi: g.dma_start(w2in[bi][:], w2t[c * 128:(c + 1) * 128, :]),
                        waits=w2in_r[bi].take())
            if LO:
                tin = G.dma(lambda g, c=c, bi=bi: g.dma_start(
                    w2in[bi][:], w2t_lo[c * 128:(c + 1) * 128, :],
                    accum_op=ALU.add), waits=[tin])
            ev_toks = []
            for nn in range(NH):
                bk = bank[6 + bi]
                def mm(t, bi=bi, nn=nn, bk=bk):
                    return t.matmul(bk.ap[:, :], h128_b[:], w2in[bi][:, nn * 512:(nn + 1) * 512],
                                    start=True, stop=True)
                tk = P.emit("t", mm, waits=[tin, t_h128] + bk.take() + (w2st_r[bi].take() if nn == 0 else []),
                            inc="ts")
                def ev(v, bi=bi, nn=nn, bk=bk):
                    return v.tensor_scalar_mul(w2st[bi][:, nn * 512:(nn + 1) * 512], bk.ap[:, :], w2scale)
                te = P.emit("v", ev, waits=[tk], inc="vs")
                bk.free.append(te)
                w2in_r[bi].free.append(te)
                ev_toks.append(te)
            def wout(q, c=c, bi=bi):
                dst = w2f_s.rearrange("(p c2) d -> p c2 d", c2=CW)[:, c, :]
                return q.dma_start(dst, w2st[bi][:])
            tout = Q.dma(wout, waits=ev_toks)
            w2st_r[bi].free.append(tout)

    def plan_wsum(w_sb, width, wid_full, dst_b, ready):
        toks = []
        nn = 0
        while nn * 512 < width:
            w = min(512, width - nn * 512)
            bk = bank[2]
            war = bk.take()
            for dc in range(DC):
                def mm(t, dc=dc, nn=nn, w=w, bk=bk):
                    return t.matmul(bk.ap[0:1, 0:w], ones_c[:],
                                    w_sb[:, dc * wid_full + nn * 512: dc * wid_full + nn * 512 + w],
                                    start=(dc == 0), stop=(dc == DC - 1))
                tk = P.emit("t", mm, waits=(ready + [t_ones] + war) if dc == 0 else [],
                            inc="ts" if dc == DC - 1 else None)
            def ev(v, nn=nn, w=w, bk=bk):
                return v.tensor_copy(dst_b[0:1, nn * 512:nn * 512 + w], bk.ap[0:1, 0:w])
            te = P.emit("v", ev, waits=[tk], inc="vs")
            bk.free.append(te)
            toks.append(te)
            nn += 1
        tb = P.emit("v", lambda v: v.drain(), waits=toks)
        return P.emit("v", lambda v: v.nop(), waits=toks, inc="vs")

    def plan_rsqrt_row(src_psum_bk, dst_row, scale, eps_ap, eps_tok, waits):
        def ln_(s):
            return s.activation(dst_row[0:1, 0:512], src_psum_bk.ap[0:1, 0:512], ACT.Ln,
                                scale=scale)
        tl = P.emit("s", ln_, waits=list(waits) + [eps_tok], inc="ss")
        src_psum_bk.free.append(tl)
        P.emit("s", lambda s: s.drain())
        def ex_(s):
            return s.activation(dst_row[0:1, 0:512], dst_row[0:1, 0:512], ACT.Exp, scale=-0.5)
        return P.emit("s", ex_, waits=[tl], inc="ss")

    def plan_rep(row_ap, rep_sb, rep_res, nrows, waits):
        bk = bank[5]
        def mm(t, bk=bk):
            return t.matmul(bk.ap[0:nrows, 0:512], onesr_f[0:1, 0:nrows], row_ap, start=True, stop=True)
        tr = P.emit("t", mm, waits=list(waits) + [t_onesr] + bk.take(), inc="ts")
        def ev(v, bk=bk):
            return v.tensor_copy(rep_sb[0:nrows, 0:512], bk.ap[0:nrows, 0:512])
        te = P.emit("v", ev, waits=[tr] + rep_res.take(), inc="vs")
        bk.free.append(te)
        return te

    def stats_mm(xsrc, sqb2buf, sqb2_rs, g, nmu_all, arow_all, off, ready, gbi, rms_w):
        bkM = bank[2]
        bkQ = bank[4]
        warM = bkM.take()
        warQ = bkQ.take()
        sq_toks = []
        for dc in range(DC):
            sb_ = sqb2buf[dc % 2]
            def sq(v, dc=dc, sb_=sb_, xsrc=xsrc):
                return v.tensor_mul(sb_[:], xsrc[:, dc * 512:(dc + 1) * 512],
                                    xsrc[:, dc * 512:(dc + 1) * 512])
            tsq = P.emit("v", sq, waits=([ready] if dc == 0 else []) + sqb2_rs[dc % 2].take(), inc="vs")
            def mmu(t, dc=dc, bkM=bkM, xsrc=xsrc):
                return t.matmul(bkM.ap[0:1, 0:512], ones_c[:], xsrc[:, dc * 512:(dc + 1) * 512],
                                start=(dc == 0), stop=(dc == DC - 1))
            P.emit("t", mmu, waits=([ready, t_ones] + warM) if dc == 0 else [])
            def msq(t, dc=dc, bkQ=bkQ, sb_=sb_):
                return t.matmul(bkQ.ap[0:1, 0:512], ones_c[:], sb_[:],
                                start=(dc == 0), stop=(dc == DC - 1))
            tms = P.emit("t", msq, waits=[tsq] + (warQ if dc == 0 else []), inc="ts")
            sqb2_rs[dc % 2].free.append(tms)
        muf = stat_muf[gbi]
        varf = stat_varf[gbi]
        mwar = stat_mv_r[gbi].take()
        def r0(v, bkM=bkM, muf=muf):
            return v.tensor_scalar_mul(muf[0:1, :], bkM.ap[0:1, 0:512], 1.0 / HID)
        t0 = P.emit("v", r0, waits=[tms] + mwar, inc="vs")
        bkM.free.append(t0)
        def nm(v, muf=muf, off=off):
            return v.tensor_scalar_mul(nmu_all[0:1, off:off + 512], muf[0:1, :], -1.0)
        tnm = P.emit("v", nm, waits=[t0], inc="vs")
        P.emit("v", lambda v: v.drain())
        def m2(v, muf=muf, varf=varf):
            return v.tensor_mul(varf[0:1, :], muf[0:1, :], muf[0:1, :])
        t2 = P.emit("v", m2, waits=[t0], inc="vs")
        P.emit("v", lambda v: v.drain())
        stat_mv_r[gbi].free.append(tnm)
        def vr(v, bkQ=bkQ, varf=varf):
            return v.scalar_tensor_tensor(varf[0:1, :], bkQ.ap[0:1, 0:512], 1.0 / HID,
                                          varf[0:1, :], ALU.mult, ALU.subtract)
        tv = P.emit("v", vr, waits=[t2], inc="vs")
        P.emit("v", lambda v: v.drain())
        bkQ.free.append(tv)
        def cr(v, varf=varf, gbi=gbi):
            return v.tensor_scalar(stat_corr[gbi][0:1, :], varf[0:1, :], 1e-5,
                                   rms_w * 1e-6, ALU.add, ALU.mult)
        tcr = P.emit("v", cr, waits=[tv], inc="vs")
        P.emit("v", lambda v: v.drain())
        ta = None
        if arow_all is not None:
            def ln_(s, varf=varf, off=off):
                return s.activation(arow_all[0:1, off:off + 512], varf[0:1, :], ACT.Ln,
                                    bias=eps5[0:1, 0:1], scale=1.0)
            tl = P.emit("s", ln_, waits=[tv, t_eps5], inc="ss")
            P.emit("s", lambda s: s.drain())
            def ex_(s, off=off):
                return s.activation(arow_all[0:1, off:off + 512],
                                    arow_all[0:1, off:off + 512], ACT.Exp, scale=-0.5)
            ta = P.emit("s", ex_, waits=[tl], inc="ss")
            stat_mv_r[gbi].free.append(ta)
        return tnm, ta, tcr

    stA1 = ExitStack()
    sa1 = lambda n, s, d: stA1.enter_context(nc.sbuf_tensor(n, s, d))
    wkva_b = sa1("wkva_b", [128, DC * KVW], BF16)
    wsumkv_b = sa1("wsumkv_b", [1, KVW], BF16)
    xbf = [sa1("xbf0", [128, DC * 512], BF16), sa1("xbf1", [128, DC * 512], BF16)]
    xbf_r = [Res(), Res()]
    nmu_all = sa1("nmu_all", [1, S], BF16)
    arow_all = sa1("arow_all", [1, S], F32)
    arep_s = sa1("arep_s", [DR, 512], F32)
    arep_r = Res()
    sqb = sa1("sqb1", [128, 512], BF16)
    sqb_r = Res()
    sqbp = [sa1("sqbp0", [128, 512], BF16), sa1("sqbp1", [128, 512], BF16)]
    sqbp_r = [Res(), Res()]
    kp_b = sa1("kp_b", [DR, 512], BF16)
    kp_r = Res()
    rope1 = sa1("rope1", [DR, 512], BF16)
    rope1_r = Res()
    rope2 = sa1("rope2", [DR, 512], BF16)
    rope2_r = Res()
    rrow = sa1("rrow1", [1, 512], F32)
    rrow_r = Res()
    rrep = sa1("rrep1", [128, 512], BF16)
    rrep_r = Res()
    coss = [sa1("coss0", [DR, 512], F32)]
    coss_r = [Res()]
    sins = [sa1("sins0", [DR, 512], F32)]
    sins_r = [Res()]

    tw_kva = G.dma(lambda g: g.dma_start(wkva_b[:], wkvaT.rearrange("(c p) q -> p c q", p=128)))
    if LO:
        tw_kva = G.dma(lambda g: g.dma_start(
            wkva_b[:], wkvaT_lo.rearrange("(c p) q -> p c q", p=128),
            accum_op=ALU.add), waits=[tw_kva])
    if cfg.get("DBG"):
        dbg_wkva2 = dram("dbg_wkva2", [128, DC * KVW], BF16, N)
        Q.dma(lambda q: q.dma_start(dbg_wkva2, wkva_b[:]), waits=[tw_kva])
    t_wsumkv = plan_wsum(wkva_b, KVW, KVW, wsumkv_b, [tw_kva])

    ckv_ready = []
    pe_ready = []
    ka_ready = [t_kones]
    xb_toks = {0: G.dma(lambda g: g.dma_start(
        xbf[0][:], xf.rearrange("(c p) t -> p c t", p=128)[:, :, 0:512]),
        waits=xbf_r[0].take())}
    for g0 in range(NG):
        bi = g0 % 2
        txb = xb_toks.pop(g0)
        if g0 + 1 < NG:
            xb_toks[g0 + 1] = G.dma(lambda g, g0=g0: g.dma_start(
                xbf[(g0 + 1) % 2][:],
                xf.rearrange("(c p) t -> p c t", p=128)[:, :, (g0 + 1) * 512:(g0 + 2) * 512]),
                waits=xbf_r[(g0 + 1) % 2].take())
        fold_step((FC // 2 + NG - 1) // NG)
        tc = Q.dma(lambda q, g0=g0: q.dma_start(coss[0][:], cosf_i[:, g0 * 512:(g0 + 1) * 512]),
                   waits=coss_r[0].take())
        if LO:
            tc = G.dma(lambda g, g0=g0: g.dma_start(
                coss[0][:], cosf_lo[:, g0 * 512:(g0 + 1) * 512],
                accum_op=ALU.add), waits=[tc])
        tsn = Q.dma(lambda q, g0=g0: q.dma_start(sins[0][:], sinf_i[:, g0 * 512:(g0 + 1) * 512]),
                    waits=sins_r[0].take())
        if LO:
            tsn = G.dma(lambda g, g0=g0: g.dma_start(
                sins[0][:], sinf_lo[:, g0 * 512:(g0 + 1) * 512],
                accum_op=ALU.add), waits=[tsn])
        if g0 == 0:
            st_next = stats_mm(xbf[0], sqbp, sqbp_r, 0, nmu_all, arow_all, 0, txb, 0, KVLR)
        tnm, ta, tcr = st_next
        if g0 + 1 < NG:
            st_next = stats_mm(xbf[(g0 + 1) % 2], sqbp, sqbp_r, g0 + 1, nmu_all, arow_all,
                               (g0 + 1) * 512, xb_toks[g0 + 1], (g0 + 1) % 2, KVLR)
        trpe = plan_rep(arow_all[0:1, g0 * 512:(g0 + 1) * 512], arep_s, arep_r, DR, [ta])

        col0 = 0
        tnrm = None
        for mc in range(KC + 1):
            mw = KVMC[mc]
            bk = bank[mc % 2]
            war = bk.take()
            for dc in range(DC):
                def mm(t, mc=mc, dc=dc, bi=bi, mw=mw, col0=col0, bk=bk):
                    return t.matmul(bk.ap[0:mw, 0:512],
                                    wkva_b[:, dc * KVW + col0: dc * KVW + col0 + mw],
                                    xbf[bi][:, dc * 512:(dc + 1) * 512],
                                    start=(dc == 0), stop=False)
                P.emit("t", mm, waits=([txb, tw_kva] + war) if dc == 0 else [])
            def fold(t, mc=mc, mw=mw, col0=col0, bk=bk, g0=g0):
                return t.matmul(bk.ap[0:mw, 0:512], wsumkv_b[0:1, col0:col0 + mw],
                                nmu_all[0:1, g0 * 512:(g0 + 1) * 512], start=False, stop=True)
            tmm = P.emit("t", fold, waits=[t_wsumkv, tnm], inc="ts")
            if mc < KC:
                def ev(v, mc=mc, g0=g0, bk=bk):
                    return v.tensor_copy(ckv_fm[:, mc * S + g0 * 512: mc * S + (g0 + 1) * 512],
                                         bk.ap[:, :])
                tev = P.emit("v", ev, waits=[tmm], inc="vs")
                bk.free.append(tev)
                if cfg.get("DBG"):
                    def dcp(v, mc=mc, g0=g0):
                        sl = slice(mc * S + g0 * 512, mc * S + (g0 + 1) * 512)
                        return v.tensor_copy(dbg_u_sb[:, sl], ckv_fm[:, sl])
                    P.emit("v", dcp, waits=[tev], inc="vs")
                def sq(v, mc=mc, g0=g0):
                    sl = ckv_fm[:, mc * S + g0 * 512: mc * S + (g0 + 1) * 512]
                    return v.tensor_mul(sqb[:], sl, sl)
                tsq = P.emit("v", sq, waits=[tev] + sqb_r.take(), inc="vs")
                bkN = bank[3]
                if mc == 0:
                    def cmm(t, bkN=bkN, g0=g0):
                        return t.matmul(bkN.ap[0:1, 0:512], ones_c[0:1, 0:1],
                                        stat_corr[g0 % 2][0:1, :], start=True, stop=False)
                    P.emit("t", cmm, waits=[tcr, t_ones] + bkN.take())
                def nmm(t, mc=mc, bkN=bkN):
                    return t.matmul(bkN.ap[0:1, 0:512], ones_c[:], sqb[:],
                                    start=False, stop=(mc == KC - 1))
                tnrm = P.emit("t", nmm, waits=[tsq], inc="ts")
                sqb_r.free.append(tnrm)
                if mc == KC - 1:
                    stat_mv_r[g0 % 2].free.append(tnrm)
            else:
                def pe(v, mw=mw, bk=bk):
                    return v.scalar_tensor_tensor(kp_b[:], bk.ap[0:DR, 0:512], 1.0,
                                                  arep_s[0:DR, :], ALU.mult, ALU.mult)
                tpe = P.emit("v", pe, waits=[tmm, trpe] + kp_r.take(), inc="vs")
                bk.free.append(tpe)
                arep_r.free.append(tpe)
            col0 += mw
        xbf_r[bi].free.append(P.tok("ts"))

        trs = plan_rsqrt_row(bank[3], rrow, 1.0 / KVLR, eps6, t_eps6, [tnrm] + rrow_r.take())
        trep = plan_rep(rrow[0:1, 0:512], rrep, rrep_r, 128, [trs])
        rrow_r.free.append(P.tok("ts"))
        for mc in range(KC):
            def ap_(v, mc=mc, g0=g0):
                sl = ckv_fm[:, mc * S + g0 * 512: mc * S + (g0 + 1) * 512]
                return v.scalar_tensor_tensor(sl, sl, gkv[:, mc:mc + 1], rrep[:], ALU.mult, ALU.mult)
            tap = P.emit("v", ap_, waits=[trep, t_gkv], inc="vs")
            P.emit("v", lambda v: v.drain())
            rrep_r.free.append(tap)
            ckv_ready.append(tap)

        bkR = bank[4]
        def rot(t, bkR=bkR):
            return t.matmul(bkR.ap[0:DR, 0:512], r64t_b[:], kp_b[:], start=True, stop=True)
        trot = P.emit("t", rot, waits=[tpe, t_r64] + bkR.take(), inc="ts")
        def rc(v):
            return v.tensor_mul(rope1[:], kp_b[:], coss[0][:])
        t1 = P.emit("v", rc, waits=[tpe, tc] + rope1_r.take(), inc="vs")
        kp_r.free.append(trot)
        def rs_(v, bkR=bkR):
            return v.scalar_tensor_tensor(rope2[:], bkR.ap[0:DR, 0:512], 1.0,
                                          sins[0][:], ALU.mult, ALU.mult)
        t2 = P.emit("v", rs_, waits=[trot, tsn] + rope2_r.take(), inc="vs")
        bkR.free.append(t2)
        def radd(v, g0=g0):
            return v.tensor_add(kaug2[0:DR, g0 * 512:(g0 + 1) * 512], rope1[:], rope2[:])
        t3 = P.emit("v", radd, waits=[t1, t2], inc="vs")
        rope1_r.free.append(t3)
        rope2_r.free.append(t3)
        kp_r.free.append(t3)
        coss_r[0].free.append(t3)
        sins_r[0].free.append(t3)
        ka_ready.append(t3)
        pe_ready.append(t3)

    barrier()
    if cfg.get("DBG"):
        dbg_xbf = dram("dbg_xbf", [128, DC * 512], BF16, N)
        dbg_wkva = dram("dbg_wkva", [128, DC * KVW], BF16, N)
        Q.dma(lambda q: q.dma_start(dbg_xbf, xbf[(S // 512 - 1) % 2][:]))
        Q.dma(lambda q: q.dma_start(dbg_wkva, wkva_b[:]))
        barrier()
    stA1.close()
    stA2 = ExitStack()
    sa2 = lambda n, s, d: stA2.enter_context(nc.sbuf_tensor(n, s, d))
    wqa_b = sa2("wqa_b", [128, DC * QLR], BF16)
    wsumq_b = sa2("wsumq_b", [1, QLR], BF16)
    xbf2 = [sa2("xbf2a", [128, DC * 512], BF16), sa2("xbf2b", [128, DC * 512], BF16)]
    xbf2_r = [Res(), Res()]
    sqbp2 = [sa2("sqbp2a", [128, 512], BF16), sa2("sqbp2b", [128, 512], BF16)]
    sqbp2_r = [Res(), Res()]
    sqb2 = sqbp2[0]
    sqb2_r = sqbp2_r[0]
    nmu2_all = sa2("nmu2_all", [1, OWN], BF16)
    rrow2 = sa2("rrow2", [1, 512], F32)
    rrow2_r = Res()
    rrep2 = sa2("rrep2", [128, 512], BF16)
    rrep2_r = Res()

    tw_qa = G.dma(lambda g: g.dma_start(wqa_b[:], wqaT.rearrange("(c p) q -> p c q", p=128)))
    if LO:
        tw_qa = G.dma(lambda g: g.dma_start(
            wqa_b[:], wqaT_lo.rearrange("(c p) q -> p c q", p=128),
            accum_op=ALU.add), waits=[tw_qa])
    t_wsumq = plan_wsum(wqa_b, QLR, QLR, wsumq_b, [tw_qa])

    cq_ready = []
    xb2_toks = {0: G.dma(lambda g: g.dma_start(
        xbf2[0][:], xof.rearrange("(c p) t -> p c t", p=128)[:, :, 0:512]),
        waits=xbf2_r[0].take())}
    for og in range(NBLK):
        bi = og % 2
        txb = xb2_toks.pop(og)
        if og + 1 < NBLK:
            xb2_toks[og + 1] = G.dma(lambda g, og=og: g.dma_start(
                xbf2[(og + 1) % 2][:],
                xof.rearrange("(c p) t -> p c t", p=128)[:, :, (og + 1) * 512:(og + 2) * 512]),
                waits=xbf2_r[(og + 1) % 2].take())
        fold_step((FC // 2 + NBLK - 1) // NBLK)
        if og == 0:
            st2_next = stats_mm(xbf2[0], sqbp2, sqbp2_r, 0, nmu2_all, None, 0, txb, 0, QLR)
        tnm, _, tcr = st2_next
        if og + 1 < NBLK:
            st2_next = stats_mm(xbf2[(og + 1) % 2], sqbp2, sqbp2_r, og + 1, nmu2_all, None,
                                   (og + 1) * 512, xb2_toks[og + 1], (og + 1) % 2, QLR)
        tnrm = None
        for mc in range(QC):
            bk = bank[mc % 2]
            war = bk.take()
            for dc in range(DC):
                def mm(t, mc=mc, dc=dc, bi=bi, bk=bk):
                    return t.matmul(bk.ap[:, 0:512],
                                    wqa_b[:, dc * QLR + mc * 128: dc * QLR + (mc + 1) * 128],
                                    xbf2[bi][:, dc * 512:(dc + 1) * 512],
                                    start=(dc == 0), stop=False)
                P.emit("t", mm, waits=([txb, tw_qa] + war) if dc == 0 else [])
            def fold(t, mc=mc, bk=bk, og=og):
                return t.matmul(bk.ap[:, 0:512], wsumq_b[0:1, mc * 128:(mc + 1) * 128],
                                nmu2_all[0:1, og * 512:(og + 1) * 512], start=False, stop=True)
            tmm = P.emit("t", fold, waits=[t_wsumq, tnm], inc="ts")
            def ev(v, mc=mc, og=og, bk=bk):
                return v.tensor_copy(cq_fm[:, mc * OWN + og * 512: mc * OWN + (og + 1) * 512],
                                     bk.ap[:, :])
            tev = P.emit("v", ev, waits=[tmm], inc="vs")
            bk.free.append(tev)
            def sq(v, mc=mc, og=og):
                sl = cq_fm[:, mc * OWN + og * 512: mc * OWN + (og + 1) * 512]
                return v.tensor_mul(sqb2[:], sl, sl)
            tsq = P.emit("v", sq, waits=[tev] + sqb2_r.take(), inc="vs")
            bkN = bank[3]
            if mc == 0:
                def cmm(t, bkN=bkN, og=og):
                    return t.matmul(bkN.ap[0:1, 0:512], ones_c[0:1, 0:1],
                                    stat_corr[og % 2][0:1, :], start=True, stop=False)
                P.emit("t", cmm, waits=[tcr, t_ones] + bkN.take())
            def nmm(t, mc=mc, bkN=bkN):
                return t.matmul(bkN.ap[0:1, 0:512], ones_c[:], sqb2[:],
                                start=False, stop=(mc == QC - 1))
            tnrm = P.emit("t", nmm, waits=[tsq], inc="ts")
            sqb2_r.free.append(tnrm)
            if mc == QC - 1:
                stat_mv_r[og % 2].free.append(tnrm)
        xbf2_r[bi].free.append(P.tok("ts"))
        trs = plan_rsqrt_row(bank[3], rrow2, 1.0 / QLR, eps6, t_eps6, [tnrm] + rrow2_r.take())
        trep = plan_rep(rrow2[0:1, 0:512], rrep2, rrep2_r, 128, [trs])
        rrow2_r.free.append(P.tok("ts"))
        for mc in range(QC):
            def ap_(v, mc=mc, og=og):
                sl = cq_fm[:, mc * OWN + og * 512: mc * OWN + (og + 1) * 512]
                return v.scalar_tensor_tensor(sl, sl, gq[:, mc:mc + 1], rrep2[:], ALU.mult, ALU.mult)
            tap = P.emit("v", ap_, waits=[trep, t_gq], inc="vs")
            P.emit("v", lambda v: v.drain())
            rrep2_r.free.append(tap)
            cq_ready.append(tap)

    fold_step(FC)
    w2f_done = Q.all_done()

    barrier()
    if cfg.get("DBG"):
        dbg_ckv = dram("dbg_ckv", [128, KC * S], BF16, N)
        dbg_kaug = dram("dbg_kaug", [DR + 1, S], BF16, N)
        dbg_cq = dram("dbg_cq", [128, QC * OWN], BF16, N)
        dbg_u = dram("dbg_u", [128, KC * S], BF16, N)
        Q.dma(lambda q: q.dma_start(dbg_ckv, ckv_fm[:]))
        Q.dma(lambda q: q.dma_start(dbg_kaug, kaug2[:]))
        Q.dma(lambda q: q.dma_start(dbg_cq, cq_fm[:]))
        Q.dma(lambda q: q.dma_start(dbg_u, dbg_u_sb[:]))
    stA2.close()
    stStats.close()
    stW2.close()
    stB = ExitStack()
    sb = lambda n, s, d: stB.enter_context(nc.sbuf_tensor(n, s, d))
    wqb_b = [sb(f"wqb_b{i}", [128, QC * 2 * DQ], BF16) for i in range(2)]
    wqb_r = [Res(), Res()]
    wkvb_b = [sb(f"wkvb_b{i}", [128, KC * KVWID], BF16) for i in range(2)]
    wkvb_r = [Res(), Res()]
    kn_b = [sb(f"kn_b{i}", [128, S], BF16) for i in range(2)]
    kn_r = [Res(), Res()]
    v_b = sb("v_b", [128, SEQT * 2 * DV], BF16)
    v_r = Res()
    qn_b = sb("qn_b", [128, OWN], BF16)
    qn_r = Res()
    qa2_b = sb("qa2_b", [DR + 1, OWN], BF16)
    qa2_r = Res()
    qpe_raw = sb("qpe_raw", [DR, 512], BF16)
    qpe_r = Res()
    ropeb1 = sb("ropeb1", [DR, 512], F32)
    ropeb1_r = Res()
    ropeb2 = sb("ropeb2", [DR, 512], F32)
    ropeb2_r = Res()
    pt_b = [sb(f"pt_b{i}", [128, 512], BF16) for i in range(3)]
    pt_r = [Res(), Res(), Res()]
    STB = (2, 3, 7)
    kn2row = [sb(f"kn2row{i}", [1, S], BF16) for i in range(2)]
    sq64b = [sb("sq64bA", [DR, 512], BF16), sb("sq64bB", [DR, 512], BF16)]
    sq64b_r = [Res(), Res()]
    kn2_r = [Res(), Res()]
    m2row = sb("m2row", [1, NBLK], F32)
    m2_r = Res()
    lnm2 = sb("lnm2", [1, NBLK], F32)
    lnm2_r = Res()
    lnq_row = sb("lnq_row", [1, 512], F32)
    lnq_r = Res()
    mrow_f = sb("mrow_f", [1, 512], F32)
    mrow_r = Res()
    rs_row = sb("rs_row", [1, 512], F32)
    rs_r = Res()
    rep_ct = sb("rep_ct", [128, 512], F32)
    repct_r = Res()
    ctx_st = sb("ctx_st", [128, 512], BF16)
    ctxst_r = Res()
    sqq = [sb("sqqA", [128, 512], BF16), sb("sqqB", [128, 512], BF16)]
    sqq_r = [Res(), Res()]
    coso = [sb(f"coso{i}", [DR, 512], F32) for i in range(2)]
    coso_r = [Res(), Res()]
    sino = [sb(f"sino{i}", [DR, 512], F32) for i in range(2)]
    sino_r = [Res(), Res()]

    mask_b = [sb(f"maskb{kd}", [128, 512 - 128 * (kd // 2)], BF16) for kd in range(8)]
    t_masks = [G.dma(lambda g, kd=kd: g.dma_start(mask_b[kd][:], mask_i[kd])) for kd in range(8)]
    cqk = _KList(cq_ready)
    kak = _KList(ka_ready)
    pend_fin = [None]
    pend_tail = [None]
    for hp in range(NPAIR):
        if pend_fin[0] is not None:
            pend_fin[0]()
            pend_fin[0] = None
        pbi = hp % 2
        twq = G.dma(lambda g, hp=hp, pbi=pbi: g.dma_start(
            wqb_b[pbi][:],
            wqbT.rearrange("(c p) q -> p c q", p=128)[:, :, hp * 2 * DQ:(hp + 1) * 2 * DQ]),
            waits=wqb_r[pbi].take())
        if LO:
            twq = G.dma(lambda g, hp=hp, pbi=pbi: g.dma_start(
                wqb_b[pbi][:],
                wqbT_lo.rearrange("(c p) q -> p c q", p=128)[:, :, hp * 2 * DQ:(hp + 1) * 2 * DQ],
                accum_op=ALU.add), waits=[twq])
        twk = G.dma(lambda g, hp=hp, pbi=pbi: g.dma_start(
            wkvb_b[pbi][:],
            wkvbT.rearrange("(c p) q -> p c q", p=128)[:, :, hp * KVWID:(hp + 1) * KVWID]),
            waits=wkvb_r[pbi].take())
        if LO:
            twk = G.dma(lambda g, hp=hp, pbi=pbi: g.dma_start(
                wkvb_b[pbi][:],
                wkvbT_lo.rearrange("(c p) q -> p c q", p=128)[:, :, hp * KVWID:(hp + 1) * KVWID],
                accum_op=ALU.add), waits=[twk])

        for hl in range(2):
            kcol = hl * (DN + DV)
            knwar = kn_r[hl].take()
            kn2war = kn2_r[hl].take()
            sqs_toks = {}

            def knorm_mms(nn, hl=hl, kn2war=kn2war):
                tsq, tspe = sqs_toks.pop(nn)
                bkN = bank[3]
                def nmm(t, nn=nn, bkN=bkN):
                    return t.matmul(bkN.ap[0:1, 0:512], ones_c[:], sqq[nn % 2][:],
                                    start=True, stop=False)
                tn1 = P.emit("t", nmm, waits=[tsq] + bkN.take(), inc="ts")
                sqq_r[nn % 2].free.append(tn1)
                def nmm2(t, nn=nn, bkN=bkN):
                    return t.matmul(bkN.ap[0:1, 0:512], ones_c[0:DR, :], sq64b[nn % 2][:],
                                    start=False, stop=True)
                tnm_ = P.emit("t", nmm2, waits=[tspe], inc="ts")
                sq64b_r[nn % 2].free.append(tnm_)
                def evk(v, hl=hl, nn=nn, bkN=bkN):
                    return v.tensor_copy(kn2row[hl][0:1, nn * 512:(nn + 1) * 512],
                                        bkN.ap[0:1, 0:512])
                tad = P.emit("v", evk, waits=[tnm_] + (kn2war if nn == 0 else []), inc="vs")
                bkN.free.append(tad)
                return tad

            for nn in range(S // 512):
                bk = bank[6 + nn % 2]
                war = bk.take() + (knwar if nn == 0 else [])
                for kc in range(KC):
                    def mm(t, hl=hl, nn=nn, kc=kc, pbi=pbi, kcol=kcol, bk=bk):
                        return t.matmul(bk.ap[:, :],
                                        wkvb_b[pbi][:, kc * KVWID + kcol: kc * KVWID + kcol + DN],
                                        ckv_fm[:, kc * S + nn * 512: kc * S + (nn + 1) * 512],
                                        start=(kc == 0), stop=(kc == KC - 1))
                    tk = P.emit("t", mm, waits=([twk] + ckv_ready + war) if kc == 0 else [],
                                inc="ts" if kc == KC - 1 else None)
                ckv_ready = []
                def ev(v, hl=hl, nn=nn, bk=bk):
                    return v.tensor_scalar_mul(kn_b[hl][:, nn * 512:(nn + 1) * 512], bk.ap[:, :], S4)
                te = P.emit("v", ev, waits=[tk], inc="vs")
                bk.free.append(te)
                def sq(v, hl=hl, nn=nn):
                    sl = kn_b[hl][:, nn * 512:(nn + 1) * 512]
                    return v.tensor_mul(sqq[nn % 2][:], sl, sl)
                tsq = P.emit("v", sq, waits=[te] + sqq_r[nn % 2].take(), inc="vs")
                def sqpe(v, nn=nn):
                    sl = kaug2[0:DR, nn * 512:(nn + 1) * 512]
                    return v.tensor_mul(sq64b[nn % 2][:], sl, sl)
                tspe = P.emit("v", sqpe, waits=pe_ready + sq64b_r[nn % 2].take(), inc="vs")
                pe_ready = []
                sqs_toks[nn] = (tsq, tspe)
                if nn > 0:
                    knorm_mms(nn - 1)
            kn2_done = knorm_mms((S // 512) - 1)

        vwar = v_r.take()
        for kt in range(SEQT):
            bk = bank[6 + kt % 2]
            war = bk.take() + (vwar if kt == 0 else [])
            for kc in range(KC):
                def mm(t, kt=kt, kc=kc, pbi=pbi, bk=bk):
                    wv = wkvb_b[pbi][:].rearrange("p (c h d) -> p c h d", h=2, d=DN + DV)
                    return t.matmul(bk.ap[:, 0:2 * DV],
                                    ckv_fm[:, kc * S + kt * 128: kc * S + (kt + 1) * 128],
                                    wv[:, kc, :, DN:DN + DV],
                                    start=(kc == 0), stop=(kc == KC - 1))
                tk = P.emit("t", mm, waits=([twk] + war) if kc == 0 else [],
                            inc="ts" if kc == KC - 1 else None)
            def ev(v, kt=kt, bk=bk):
                return v.tensor_copy(v_b[:, kt * 2 * DV:(kt + 1) * 2 * DV], bk.ap[:, 0:2 * DV])
            te = P.emit("v", ev, waits=[tk], inc="vs")
            bk.free.append(te)
        v_done = te

        for hl in range(2):
            h = hp * 2 + hl
            qcol = hl * DQ
            m2war = m2_r.take() + lnm2_r.take()
            prev = None
            for j in range(NBLK):
                def rm(v, j=j, hl=hl):
                    return v.reduce_max(m2row[0:1, j:j + 1], kn2row[hl][0:1, j * 1024:(j + 1) * 1024],
                                        axis=AX.X)
                tm = P.emit("v", rm, waits=[kn2_done] + (m2war if j == 0 else []), inc="vs")
                P.emit("v", lambda v: v.drain())
                if j > 0:
                    def cmb(v, j=j):
                        return v.tensor_max(m2row[0:1, j:j + 1], m2row[0:1, j:j + 1], m2row[0:1, j - 1:j])
                    tm = P.emit("v", cmb, inc="vs")
                    P.emit("v", lambda v: v.drain())
            def lnm(s):
                return s.activation(lnm2[0:1, :], m2row[0:1, :], ACT.Ln)
            tlnm = P.emit("s", lnm, waits=[tm], inc="ss")
            m2_r.free.append(tlnm)

            qn_war = qn_r.take()
            qa2_war = qa2_r.take()

            def emit_prep(og, hl=hl, pbi=pbi, qcol=qcol, tlnm=tlnm):
                cbi = og % 2
                tc = Q.dma(lambda q, og=og, cbi=cbi: q.dma_start(
                    coso[cbi][:], cosfo_i[:, og * 512:(og + 1) * 512]), waits=coso_r[cbi].take())
                if LO:
                    tc = G.dma(lambda g, og=og, cbi=cbi: g.dma_start(
                        coso[cbi][:], cosfo_lo[:, og * 512:(og + 1) * 512],
                        accum_op=ALU.add), waits=[tc])
                tsn = Q.dma(lambda q, og=og, cbi=cbi: q.dma_start(
                    sino[cbi][:], sinfo_i[:, og * 512:(og + 1) * 512]), waits=sino_r[cbi].take())
                if LO:
                    tsn = G.dma(lambda g, og=og, cbi=cbi: g.dma_start(
                        sino[cbi][:], sinfo_lo[:, og * 512:(og + 1) * 512],
                        accum_op=ALU.add), waits=[tsn])
                bk = bank[0]
                war = bk.take() + (qn_war if og == 0 else [])
                for qc in range(QC):
                    def mm(t, og=og, qc=qc, pbi=pbi, qcol=qcol, bk=bk):
                        return t.matmul(bk.ap[:, :],
                                        wqb_b[pbi][:, qc * 2 * DQ + qcol: qc * 2 * DQ + qcol + DN],
                                        cq_fm[:, qc * OWN + og * 512: qc * OWN + (og + 1) * 512],
                                        start=(qc == 0), stop=(qc == QC - 1))
                    tk = P.emit("t", mm, waits=([twq] + cqk.pop() + war) if qc == 0 else [],
                                inc="ts" if qc == QC - 1 else None)
                def ev(v, og=og, bk=bk):
                    return v.tensor_scalar_mul(qn_b[:, og * 512:(og + 1) * 512], bk.ap[:, :], S4)
                te = P.emit("v", ev, waits=[tk], inc="vs")
                bk.free.append(te)
                bk1 = bank[1]
                war1 = bk1.take()
                for qc in range(QC):
                    def mm(t, og=og, qc=qc, pbi=pbi, qcol=qcol, bk1=bk1):
                        return t.matmul(bk1.ap[0:DR, 0:512],
                                        wqb_b[pbi][:, qc * 2 * DQ + qcol + DN: qc * 2 * DQ + qcol + DQ],
                                        cq_fm[:, qc * OWN + og * 512: qc * OWN + (og + 1) * 512],
                                        start=(qc == 0), stop=(qc == QC - 1))
                    tk1 = P.emit("t", mm, waits=war1 if qc == 0 else [],
                                 inc="ts" if qc == QC - 1 else None)
                def sq(v, og=og):
                    sl = qn_b[:, og * 512:(og + 1) * 512]
                    return v.tensor_mul(sqq[0][:], sl, sl)
                tsq = P.emit("v", sq, waits=[te] + sqq_r[0].take(), inc="vs")
                bkN = bank[0]
                def nmm(t, bkN=bkN):
                    return t.matmul(bkN.ap[0:1, 0:512], ones_c[:], sqq[0][:], start=True, stop=False)
                tqn1 = P.emit("t", nmm, waits=[tsq] + bkN.take(), inc="ts")
                sqq_r[0].free.append(tqn1)
                def evp(v, bk1=bk1):
                    return v.tensor_copy(qpe_raw[:], bk1.ap[0:DR, 0:512])
                tep = P.emit("v", evp, waits=[tk1] + qpe_r.take(), inc="vs")
                bk1.free.append(tep)
                bkR = bank[1]
                def rot(t, bkR=bkR):
                    return t.matmul(bkR.ap[0:DR, 0:512], r64t_b[:], qpe_raw[:], start=True, stop=True)
                trot = P.emit("t", rot, waits=[tep, t_r64] + bkR.take(), inc="ts")
                qpe_r.free.append(trot)
                def rc(v, cbi=cbi):
                    return v.tensor_mul(ropeb1[:], qpe_raw[:], coso[cbi][:])
                t1 = P.emit("v", rc, waits=[tep, tc] + ropeb1_r.take(), inc="vs")
                qpe_r.free.append(t1)
                def rs_(v, cbi=cbi, bkR=bkR):
                    return v.scalar_tensor_tensor(ropeb2[:], bkR.ap[0:DR, 0:512], 1.0,
                                                  sino[cbi][:], ALU.mult, ALU.mult)
                t2 = P.emit("v", rs_, waits=[trot, tsn] + ropeb2_r.take(), inc="vs")
                bkR.free.append(t2)
                def radd(v, og=og):
                    return v.tensor_add(qa2_b[0:DR, og * 512:(og + 1) * 512], ropeb1[:], ropeb2[:])
                t3 = P.emit("v", radd, waits=[t1, t2] + (qa2_war if og == 0 else []), inc="vs")
                ropeb1_r.free.append(t3)
                ropeb2_r.free.append(t3)
                coso_r[cbi].free.append(t3)
                sino_r[cbi].free.append(t3)
                def tail(og=og, bkN=bkN, t3=t3, cbi=cbi):
                    return prep_tail(og, bkN, t3, cbi)
                return te, tail

            def prep_tail(og, bkN, t3, cbi, hl=hl, tlnm=tlnm):
                def sqp(v, og=og):
                    sl = qa2_b[0:DR, og * 512:(og + 1) * 512]
                    return v.tensor_mul(sqq[0][0:DR, :], sl, sl)
                tqp = P.emit("v", sqp, waits=[t3] + sqq_r[0].take(), inc="vs")
                def nmm2(t, bkN=bkN):
                    return t.matmul(bkN.ap[0:1, 0:512], ones_c[0:DR, :], sqq[0][0:DR, :],
                                    start=False, stop=True)
                tq2 = P.emit("t", nmm2, waits=[tqp], inc="ts")
                sqq_r[0].free.append(tq2)
                def lnq(s, bkN=bkN):
                    return s.activation(lnq_row[0:1, 0:512], bkN.ap[0:1, 0:512], ACT.Ln)
                tlq = P.emit("s", lnq, waits=[tq2, tlnm] + lnq_r.take(), inc="ss")
                bkN.free.append(tlq)
                def addm(v, og=og):
                    return v.tensor_scalar(mrow_f[0:1, 0:512], lnq_row[0:1, 0:512],
                                           lnm2[0:1, og:og + 1], None, ALU.add)
                tam = P.emit("v", addm, waits=[tlq] + mrow_r.take(), inc="vs")
                lnq_r.free.append(tam)
                lnm2_r.free.append(tam)
                def exm(s):
                    return s.activation(mrow_f[0:1, 0:512], mrow_f[0:1, 0:512], ACT.Exp, scale=0.5)
                tem = P.emit("s", exm, waits=[tam], inc="ss")
                def negm(v, og=og):
                    return v.tensor_scalar_mul(qa2_b[DR:DR + 1, og * 512:(og + 1) * 512],
                                               mrow_f[0:1, 0:512], -1.0)
                tnegm = P.emit("v", negm, waits=[tem], inc="vs")
                mrow_r.free.append(tnegm)
                return tnegm

            def emit_attn(og, te, tnegm, hl=hl):
                bkCT = bank[4 + 2 * (og % 2)]
                bkS = bank[5]
                kts = list(range(8 * og)) + [("d", kd) for kd in range(8)]
                nkt = len(kts)

                def _geo(item):
                    if isinstance(item, tuple):
                        kd = item[1]
                        return 8 * og + kd, kd, 512 - 128 * (kd // 2), 128 * (kd // 2)
                    return item, None, 512, 0

                def emit_scores(item, idx, hl=hl, og=og, te=te, v_done=v_done, tnegm=tnegm):
                    kt, kd, W, coff = _geo(item)
                    sbk = bank[STB[idx % 3]]
                    swar = sbk.take()
                    def mm1(t, hl=hl, kt=kt, og=og, W=W, coff=coff, sbk=sbk):
                        return t.matmul(sbk.ap[:, 0:W],
                                        kn_b[hl][:, kt * 128:(kt + 1) * 128],
                                        qn_b[:, og * 512 + coff: og * 512 + coff + W],
                                        start=True, stop=False)
                    P.emit("t", mm1, waits=[te, v_done] + swar)
                    def mm2(t, kt=kt, og=og, W=W, coff=coff, sbk=sbk):
                        return t.matmul(sbk.ap[:, 0:W],
                                        kaug2[:, kt * 128:(kt + 1) * 128],
                                        qa2_b[:, og * 512 + coff: og * 512 + coff + W],
                                        start=False, stop=True)
                    return P.emit("t", mm2, waits=[tnegm] + kar.pop(), inc="ts")

                def emit_px(item, idx, tmm2):
                    kt, kd, W, coff = _geo(item)
                    sbk = bank[STB[idx % 3]]
                    pp = idx % 3
                    def ex_(s, W=W, pp=pp, sbk=sbk):
                        return s.activation(pt_b[pp][:, 0:W], sbk.ap[:, 0:W], ACT.Exp)
                    tex = P.emit("s", ex_, waits=[tmm2] + pt_r[pp].take(), inc="ss")
                    sbk.free.append(tex)
                    if kd is None:
                        return tex
                    def msk(v, kd=kd, W=W, pp=pp):
                        return v.tensor_mul(pt_b[pp][:, 0:W], pt_b[pp][:, 0:W], mask_b[kd][:, 0:W])
                    tmsk = P.emit("v", msk, waits=[tex, t_masks[kd]], inc="vs")
                    P.emit("v", lambda v: v.drain())
                    return tmsk

                def emit_av(item, idx, av_wait, hl=hl, bkCT=bkCT, bkS=bkS):
                    kt, kd, W, coff = _geo(item)
                    pp = idx % 3
                    first = idx == 0
                    last = idx == nkt - 1
                    def av(t, hl=hl, kt=kt, W=W, coff=coff, pp=pp, first=first, last=last, bkCT=bkCT):
                        return t.matmul(bkCT.ap[0:DV, coff:coff + W],
                                        v_b[:, kt * 2 * DV + hl * DV: kt * 2 * DV + (hl + 1) * DV],
                                        pt_b[pp][:, 0:W], start=first, stop=last)
                    P.emit("t", av, waits=[av_wait] + (ct_war if first else []))
                    def sm(t, W=W, coff=coff, pp=pp, first=first, last=last, bkS=bkS):
                        return t.matmul(bkS.ap[0:1, coff:coff + W], ones_c[:],
                                        pt_b[pp][:, 0:W], start=first, stop=last)
                    tsm = P.emit("t", sm, waits=(s_war if first else []), inc="ts")
                    pt_r[pp].free.append(tsm)
                    return tsm

                kar = kak
                tmm2s = {0: emit_scores(kts[0], 0)}
                if nkt > 1:
                    tmm2s[1] = emit_scores(kts[1], 1)
                if pend_fin[0] is not None:
                    pend_fin[0]()
                    pend_fin[0] = None
                if pend_tail[0] is not None:
                    og1, te1, tail1 = pend_tail[0]
                    pend_tail[0] = None
                    toks[og1] = (te1, tail1())
                ct_war = bkCT.take()
                s_war = bkS.take()
                for idx, item in enumerate(kts):
                    avw = emit_px(item, idx, tmm2s.pop(idx))
                    if idx + 2 < nkt:
                        tmm2s[idx + 2] = emit_scores(kts[idx + 2], idx + 2)
                    tsm = emit_av(item, idx, avw)
                ct_done = tsm
                def rcp(v, bkS=bkS):
                    return v.reciprocal(rs_row[0:1, 0:512], bkS.ap[0:1, 0:512])
                trc = P.emit("v", rcp, waits=[ct_done] + rs_r.take(), inc="vs")
                bkS.free.append(trc)

                def fin(bkCT=bkCT, ct_done=ct_done, og=og, h=h, trc=trc):
                    trep = plan_rep(rs_row[0:1, 0:512], rep_ct, repct_r, 128, [trc])
                    rs_r.free.append(P.tok("ts"))
                    def ctxev(v, bkCT=bkCT):
                        return v.scalar_tensor_tensor(ctx_st[0:DV, 0:512], bkCT.ap[0:DV, 0:512], 1.0,
                                                      rep_ct[0:DV, 0:512], ALU.mult, ALU.mult)
                    tcv = P.emit("v", ctxev, waits=[ct_done, trep] + ctxst_r.take(), inc="vs")
                    bkCT.free.append(tcv)
                    repct_r.free.append(tcv)
                    tdma = Q.dma(lambda q, h=h, og=og: q.dma_start(
                        ctx_s[h * DV:(h + 1) * DV, og * 512:(og + 1) * 512], ctx_st[0:DV, 0:512]),
                        waits=[tcv])
                    ctxst_r.free.append(tdma)

                pend_fin[0] = fin
                qn_r.free.append(P.tok("ts"))
                qa2_r.free.append(P.tok("ts"))

            te0, tail0 = emit_prep(0)
            toks = {0: (te0, tail0())}
            for og in range(NBLK):
                if og + 1 < NBLK:
                    te1, tail1 = emit_prep(og + 1)
                    pend_tail[0] = (og + 1, te1, tail1)
                emit_attn(og, *toks[og])
                if pend_tail[0] is not None:
                    og1, te1, tail1 = pend_tail[0]
                    pend_tail[0] = None
                    toks[og1] = (te1, tail1())
            kn_r[hl].free.append(P.tok("ts"))
            kn2_r[hl].free.append(tlnm)
            lnm2_r.free.append(P.tok("vs"))
        v_r.free.append(P.tok("ts"))
        wqb_r[pbi].free.append(P.tok("ts"))
        wkvb_r[pbi].free.append(P.tok("ts"))

    if pend_fin[0] is not None:
        pend_fin[0]()
        pend_fin[0] = None
    ctx_all_done = Q.all_done()

    barrier()
    stB.close()
    stAB.close()
    stB2 = ExitStack()
    sb2 = lambda n, s, d: stB2.enter_context(nc.sbuf_tensor(n, s, d))
    wo_b = sb2("wo_b", [128, DC * HID], BF16)
    ctxg = sb2("ctxg", [128, (H * DV // 128) * 512], BF16)
    ctxg_r = Res()
    xog = sb2("xog", [128, DC * 512], F32)
    xog_r = Res()
    y1g = sb2("y1g", [128, DC * 512], BF16)
    y1g_r = Res()
    y1st = sb2("y1st", [128, 512], F32)
    y1st_r = Res()
    sqs = sb2("sqs", [128, 512], BF16)
    sqs_r = Res()
    var_row = sb2("var_row", [1, 512], F32)
    mu2_row = sb2("mu2_row", [1, 512], F32)
    rowsb_r = Res()
    mu_row = sb2("mu_row", [1, 512], F32)
    a2_row = sb2("a2_row", [1, 512], F32)
    a2rep = sb2("a2rep", [128, 512], F32)
    a2rep_r = Res()
    murep = sb2("murep", [128, 512], F32)
    murep_r = Res()
    nrm_t = sb2("nrm_t", [128, 512], F32)
    nrm_r = Res()
    y1n_st = sb2("y1n_st", [128, 512], BF16)
    y1n_r = Res()

    two = G.dma(lambda g: g.dma_start(wo_b[:], woT.rearrange("(c p) q -> p c q", p=128)))
    if LO:
        two = G.dma(lambda g: g.dma_start(
            wo_b[:], woT_lo.rearrange("(c p) q -> p c q", p=128),
            accum_op=ALU.add), waits=[two])
    HC = H * DV // 128
    for tg in range(OWN // 512):
        tctx = SQ.dma(lambda q, tg=tg: q.dma_start(
            ctxg[:], ctx_s.rearrange("(c p) t -> p c t", p=128)[:, :, tg * 512:(tg + 1) * 512]),
            waits=[x for x in ctx_all_done] + ctxg_r.take())
        ctx_all_done = []
        txo = SQ.dma(lambda q, tg=tg: q.dma_start(
            xog[:], xof.rearrange("(c p) t -> p c t", p=128)[:, :, tg * 512:(tg + 1) * 512]),
            waits=xog_r.take())
        y1war = y1g_r.take()
        bkMu = bank[4]
        bkSq = bank[5]
        mu_war = bkMu.take()
        sq_war = bkSq.take()
        for dc in range(DC):
            bk = bank[dc % 4]
            war = bk.take()
            for hc in range(HC):
                def mm(t, dc=dc, hc=hc, bk=bk):
                    return t.matmul(bk.ap[:, :],
                                    wo_b[:, hc * HID + dc * 128: hc * HID + (dc + 1) * 128],
                                    ctxg[:, hc * 512:(hc + 1) * 512],
                                    start=(hc == 0), stop=(hc == HC - 1))
                tk = P.emit("t", mm, waits=([two, tctx] + war) if hc == 0 else [],
                            inc="ts" if hc == HC - 1 else None)
            def evf(v, dc=dc, bk=bk):
                return v.tensor_add(y1st[:], bk.ap[:, :], xog[:, dc * 512:(dc + 1) * 512])
            tef = P.emit("v", evf, waits=[tk, txo] + y1st_r.take(), inc="vs")
            bk.free.append(tef)
            td1 = Q.dma(lambda q, tg=tg, dc=dc: q.dma_start(
                y1_s[dc * 128:(dc + 1) * 128, tg * 512:(tg + 1) * 512], y1st[:]), waits=[tef])
            def evb(v, dc=dc):
                return v.tensor_copy(y1g[:, dc * 512:(dc + 1) * 512], y1st[:])
            teb = P.emit("v", evb, waits=[tef] + (y1war if dc == 0 else []), inc="vs")
            y1st_r.free.append(td1)
            y1st_r.free.append(teb)
            def mmu(t, dc=dc, bkMu=bkMu):
                return t.matmul(bkMu.ap[0:1, 0:512], ones_c[:], y1g[:, dc * 512:(dc + 1) * 512],
                                start=(dc == 0), stop=(dc == DC - 1))
            tmu = P.emit("t", mmu, waits=[teb] + (mu_war if dc == 0 else []), inc="ts")
            def sq_(v, dc=dc):
                sl = y1g[:, dc * 512:(dc + 1) * 512]
                return v.tensor_mul(sqs[:], sl, sl)
            tsq = P.emit("v", sq_, waits=[teb] + sqs_r.take(), inc="vs")
            def msq(t, dc=dc, bkSq=bkSq):
                return t.matmul(bkSq.ap[0:1, 0:512], ones_c[:], sqs[:],
                                start=(dc == 0), stop=(dc == DC - 1))
            tms = P.emit("t", msq, waits=[tsq] + (sq_war if dc == 0 else []), inc="ts")
            sqs_r.free.append(tms)
        rb_war = rowsb_r.take()
        def r0(v, bkMu=bkMu):
            return v.tensor_scalar_mul(mu_row[0:1, :], bkMu.ap[0:1, 0:512], 1.0 / HID)
        t0 = P.emit("v", r0, waits=[tmu] + rb_war, inc="vs")
        bkMu.free.append(t0)
        def r1(v, bkSq=bkSq):
            return v.tensor_scalar_mul(var_row[0:1, :], bkSq.ap[0:1, 0:512], 1.0 / HID)
        t1_ = P.emit("v", r1, waits=[tms], inc="vs")
        bkSq.free.append(t1_)
        P.emit("v", lambda v: v.drain())
        def r2(v):
            return v.tensor_mul(mu2_row[0:1, :], mu_row[0:1, :], mu_row[0:1, :])
        t2_ = P.emit("v", r2, waits=[t0], inc="vs")
        P.emit("v", lambda v: v.drain())
        def r3(v):
            return v.tensor_sub(var_row[0:1, :], var_row[0:1, :], mu2_row[0:1, :])
        t3_ = P.emit("v", r3, waits=[t1_, t2_], inc="vs")
        P.emit("v", lambda v: v.drain())
        def lnv(s):
            return s.activation(a2_row[0:1, :], var_row[0:1, :], ACT.Ln, bias=eps5[0:1, 0:1])
        t4 = P.emit("s", lnv, waits=[t3_, t_eps5], inc="ss")
        P.emit("s", lambda s: s.drain())
        def exv(s):
            return s.activation(a2_row[0:1, :], a2_row[0:1, :], ACT.Exp, scale=-0.5)
        t5 = P.emit("s", exv, waits=[t4], inc="ss")
        ta2 = plan_rep(a2_row[0:1, 0:512], a2rep, a2rep_r, 128, [t5])
        tmu_ = plan_rep(mu_row[0:1, 0:512], murep, murep_r, 128, [t0])
        rowsb_r.free.append(P.tok("ts"))
        for dc in range(DC):
            def n1(v, dc=dc):
                return v.tensor_sub(nrm_t[:], y1g[:, dc * 512:(dc + 1) * 512], murep[:])
            tn1 = P.emit("v", n1, waits=[ta2, tmu_] + nrm_r.take(), inc="vs")
            def n2(v):
                return v.tensor_mul(y1n_st[:], nrm_t[:], a2rep[:])
            tn2 = P.emit("v", n2, waits=[tn1] + y1n_r.take(), inc="vs")
            nrm_r.free.append(tn2)
            tnd = Q.dma(lambda q, tg=tg, dc=dc: q.dma_start(
                y1n_s[dc * 128:(dc + 1) * 128, tg * 512:(tg + 1) * 512], y1n_st[:]), waits=[tn2])
            y1n_r.free.append(tnd)
            y1g_r.free.append(tn2)
        a2rep_r.free.append(P.tok("vs"))
        murep_r.free.append(P.tok("vs"))
        y1g_r.free.append(P.tok("ts"))
        ctxg_r.free.append(P.tok("ts"))
        xog_r.free.append(P.tok("vs"))

    y1_done = Q.all_done()

    barrier()
    stB2.close()
    stC = ExitStack()
    Hbuf = stC.enter_context(nc.sbuf_tensor("Hbuf", [128, FC * GRP], BF16))
    Hbuf_r = Res()

    DQW = 256 if HID >= 2048 else 128
    NDQ = HID // DQW
    for fg in range(NFG):
        if fg:
            barrier()
        st1 = ExitStack()
        s1 = lambda n, s, d: st1.enter_context(nc.sbuf_tensor(n + str(fg), s, d))
        lnf = s1("lnf", [128, DC * GRP], BF16)
        w1s = [s1(f"w1s{i}", [128, DC * 128], BF16) for i in range(2)]
        w1s_r = [Res(), Res()]
        w3s = [s1(f"w3s{i}", [128, DC * 128], BF16) for i in range(2)]
        w3s_r = [Res(), Res()]
        sil = s1("sil", [128, 512], BF16)
        sil_r = Res()

        tln = G.dma(lambda g, fg=fg, lnf=lnf: g.dma_start(
            lnf[:], y1n_s.rearrange("(c p) t -> p c t", p=128)[:, :, fg * GRP:(fg + 1) * GRP]),
            waits=(y1_done if fg == 0 else []))
        hbuf_war = Hbuf_r.take()
        hm_toks = {}
        h1_done = []

        def emit_h1(j):
            for nn in range(GN):
                bk = bank[4 + (j + nn) % 2]
                war = bk.take()
                def mm(t, j=j, nn=nn, bk=bk):
                    return t.matmul(bk.ap[:, :], m64_b[:],
                                    Hbuf[:, j * GRP + nn * 512: j * GRP + nn * 512 + 512],
                                    start=True, stop=True)
                tk = P.emit("t", mm, waits=[hm_toks[j], t_m64] + war, inc="ts")
                def ev(v, j=j, nn=nn, bk=bk):
                    return v.tensor_copy(Hbuf[:, j * GRP + nn * 512: j * GRP + nn * 512 + 512],
                                         bk.ap[:, :])
                te = P.emit("v", ev, waits=[tk], inc="vs")
                bk.free.append(te)
                h1_done.append(te)

        for fc in range(FC):
            wbi = fc % 2
            tw1 = G.dma(lambda g, fc=fc, wbi=wbi, w1s=w1s: g.dma_start(
                w1s[wbi][:], w1pt.rearrange("(c p) f -> p c f", p=128)[:, :, fc * 128:(fc + 1) * 128]),
                waits=w1s_r[wbi].take())
            if LO:
                tw1 = G.dma(lambda g, fc=fc, wbi=wbi, w1s=w1s: g.dma_start(
                    w1s[wbi][:],
                    w1pt_lo.rearrange("(c p) f -> p c f", p=128)[:, :, fc * 128:(fc + 1) * 128],
                    accum_op=ALU.add), waits=[tw1])
            tw3 = G.dma(lambda g, fc=fc, wbi=wbi, w3s=w3s: g.dma_start(
                w3s[wbi][:], w3pt.rearrange("(c p) f -> p c f", p=128)[:, :, fc * 128:(fc + 1) * 128]),
                waits=w3s_r[wbi].take())
            if LO:
                tw3 = G.dma(lambda g, fc=fc, wbi=wbi, w3s=w3s: g.dma_start(
                    w3s[wbi][:],
                    w3pt_lo.rearrange("(c p) f -> p c f", p=128)[:, :, fc * 128:(fc + 1) * 128],
                    accum_op=ALU.add), waits=[tw3])
            for nn in range(GN):
                bkG = bank[(2 * (fc % 2)) % 4]
                bkL = bank[(2 * (fc % 2) + 1) % 4]
                warG = bkG.take()
                warL = bkL.take()
                for dc in range(DC):
                    def mmg(t, fc=fc, nn=nn, dc=dc, wbi=wbi, bkG=bkG, w1s=w1s, lnf=lnf):
                        return t.matmul(bkG.ap[:, :],
                                        w1s[wbi][:, dc * 128:(dc + 1) * 128],
                                        lnf[:, dc * GRP + nn * 512: dc * GRP + nn * 512 + 512],
                                        start=(dc == 0), stop=(dc == DC - 1))
                    tg_ = P.emit("t", mmg, waits=([tw1, tln] + warG) if dc == 0 else [],
                                 inc="ts" if dc == DC - 1 else None)
                for dc in range(DC):
                    def mml(t, fc=fc, nn=nn, dc=dc, wbi=wbi, bkL=bkL, w3s=w3s, lnf=lnf):
                        return t.matmul(bkL.ap[:, :],
                                        w3s[wbi][:, dc * 128:(dc + 1) * 128],
                                        lnf[:, dc * GRP + nn * 512: dc * GRP + nn * 512 + 512],
                                        start=(dc == 0), stop=(dc == DC - 1))
                    tl_ = P.emit("t", mml, waits=([tw3] + warL) if dc == 0 else [],
                                 inc="ts" if dc == DC - 1 else None)
                def sl_(s, bkG=bkG, sil=sil):
                    return s.activation(sil[:], bkG.ap[:, :], ACT.Silu)
                tsl = P.emit("s", sl_, waits=[tg_] + sil_r.take(), inc="ss")
                bkG.free.append(tsl)
                def hm(v, fc=fc, nn=nn, bkL=bkL, sil=sil):
                    return v.scalar_tensor_tensor(Hbuf[:, fc * GRP + nn * 512: fc * GRP + nn * 512 + 512],
                                                  bkL.ap[:, :], 1.0, sil[:], ALU.mult, ALU.mult)
                thm = P.emit("v", hm, waits=[tl_, tsl] + (hbuf_war if (fc == 0 and nn == 0) else []), inc="vs")
                bkL.free.append(thm)
                sil_r.free.append(thm)
                hm_toks[fc] = thm
            w1s_r[wbi].free.append(P.tok("ts"))
            w3s_r[wbi].free.append(P.tok("ts"))
            if fc >= 2:
                emit_h1(fc - 2)
        emit_h1(FC - 2)
        emit_h1(FC - 1)
        st1.close()
        stage_toks = [P.tok("ts"), P.tok("vs"), P.tok("ss")]

        st2 = ExitStack()
        s2 = lambda n, s, d: st2.enter_context(nc.sbuf_tensor(n + str(fg), s, d))
        w2b = [s2(f"w2b{i}", [128, FC * DQW], BF16) for i in range(2)]
        w2b_r = [Res(), Res()]
        y1r = [s2(f"y1r{i}", [128, 512], F32) for i in range(2)]
        y1r_r = [Res(), Res()]
        outst = [s2(f"outst{i}", [128, 512], F32) for i in range(2)]
        outst_r = [Res(), Res()]
        for dq in range(NDQ):
            wbi = dq % 2
            ld_toks = []
            war = w2b_r[wbi].take()
            def ld(q, dq=dq, wbi=wbi, w2b=w2b):
                v = w2f_s.rearrange("(j q) d -> q j d", q=128)
                return q.dma_start(
                    w2b[wbi][:].rearrange("q (j d) -> q j d", d=DQW),
                    v[:, :, dq * DQW:(dq + 1) * DQW])
            tk = SQ.dma(ld, waits=war + w2f_done + stage_toks)
            ld_toks.append(tk)
            for nn in range(GN):
                for mc in range(DQW // 128):
                    bk = bank[(nn * 2 + mc) % 4]
                    warb = bk.take()
                    ty1 = SQ.dma(lambda q, fg=fg, dq=dq, nn=nn, mc=mc, wbi=wbi, y1r=y1r: q.dma_start(
                        y1r[wbi][:], y1_s[dq * DQW + mc * 128: dq * DQW + (mc + 1) * 128,
                                          fg * GRP + nn * 512: fg * GRP + nn * 512 + 512]),
                        waits=y1r_r[wbi].take() + (y1_done if (fg == 0 and dq == 0 and nn == 0 and mc == 0) else []))
                    for j in range(FC):
                        def mm(t, dq=dq, nn=nn, mc=mc, j=j, wbi=wbi, bk=bk, w2b=w2b):
                            return t.matmul(bk.ap[:, :],
                                            w2b[wbi][:, j * DQW + mc * 128: j * DQW + (mc + 1) * 128],
                                            Hbuf[:, j * GRP + nn * 512: j * GRP + nn * 512 + 512],
                                            start=(j == 0), stop=(j == FC - 1))
                        tk = P.emit("t", mm,
                                    waits=(ld_toks + h1_done + warb) if j == 0 else [],
                                    inc="ts" if j == FC - 1 else None)
                    h1_done = []
                    def ev(v, wbi=wbi, bk=bk, y1r=y1r, outst=outst):
                        return v.tensor_add(outst[wbi][:], bk.ap[:, :], y1r[wbi][:])
                    te = P.emit("v", ev, waits=[tk, ty1] + outst_r[wbi].take(), inc="vs")
                    bk.free.append(te)
                    y1r_r[wbi].free.append(te)
                    tod = Q.dma(lambda q, fg=fg, dq=dq, nn=nn, mc=mc, wbi=wbi, outst=outst: q.dma_start(
                        out[dq * DQW + mc * 128: dq * DQW + (mc + 1) * 128,
                            fg * GRP + nn * 512: fg * GRP + nn * 512 + 512], outst[wbi][:]),
                        waits=[te])
                    outst_r[wbi].free.append(tod)
            w2b_r[wbi].free.append(P.tok("ts"))
        Hbuf_r.free.append(P.tok("ts"))
        st2.close()

    def fin(q):
        return None
    final_waits = Q.all_done() + G.all_done() + SQ.all_done()
    P.emit("q", None, waits=final_waits)

    with nc.Block() as block:
        def runner(eng, name):
            last_wait = {}
            for fn, waits, inc, amt in P.ops[name]:
                for s, v in waits:
                    if last_wait.get(s, -1) < v:
                        eng.wait_ge(sems[s], v)
                        last_wait[s] = v
                ins = fn(eng) if fn is not None else None
                if inc is not None:
                    assert ins is not None, f"inc {inc} needs an instruction on {name}"
                    ins.then_inc(sems[inc], amt)

        @block.sync
        def _(q):
            runner(q, "q")

        @block.gpsimd
        def _(g):
            runner(g, "g")

        @block.tensor
        def _(t):
            runner(t, "t")

        @block.vector
        def _(v):
            runner(v, "v")

        @block.scalar
        def _(s):
            runner(s, "s")

    stC.close()
    gst.close()
    return nc


def _prep_inputs(cfg, x_full, w, b, r):
    own = own_tokens(cfg, r)
    xb = x_full[b]
    c = host_consts(cfg, r)
    perm = w1_perm(cfg["FFNH"])
    d = {
        "xf": np.ascontiguousarray(xb.T),
        "xt": np.ascontiguousarray(xb),
        "xof": np.ascontiguousarray(xb[own].T),
        "xot": np.ascontiguousarray(xb[own]),
        "wqaT": w["wqaT"], "wkvaT": w["wkvaT"], "wqbT": w["wqbT"],
        "wkvbT": w["wkvbT"], "woT": w["woT"], "w1pt": w["w1pt"],
        "w3pt": w["w3pt"], "w2t": w["w2t"], "gq": w["gq"], "gkv": w["gkv"],
        "h128": c["h128"], "m64": c["m64"], "r64t": c["r64t"], "idenf": c["idenf"],
        "cosf": c["cosf"], "sinf": c["sinf"], "cosfo": c["cosfo"], "sinfo": c["sinfo"],
    }
    if cfg.get("LO"):
        d.update({
            "wqaT_lo": w["wqaT_lo"], "wkvaT_lo": w["wkvaT_lo"],
            "wqbT_lo": w["wqbT_lo"], "wkvbT_lo": w["wkvbT_lo"],
            "woT_lo": w["woT_lo"], "w1pt_lo": w["w1pt_lo"],
            "w3pt_lo": w["w3pt_lo"], "w2t_lo": w["w2t_lo"],
            "cosf_lo": c["cosf_lo"], "sinf_lo": c["sinf_lo"],
            "cosfo_lo": c["cosfo_lo"], "sinfo_lo": c["sinfo_lo"],
        })
    for kd in range(8):
        d[f"mask{kd}"] = c[f"mask{kd}"]
    return d, own


def _hilo(a):
    a64 = np.asarray(a, np.float64)
    hi = a64.astype(np.float32)
    lo = (a64 - hi).astype(np.float32)
    return hi, lo


def _shared_weights(cfg, L0_attn_w_qa, L0_attn_q_a_ln_w, L0_attn_w_qb, L0_attn_w_kva,
                    L0_attn_kv_a_ln_w, L0_attn_w_kvb, L0_attn_w_o, L0_ffn_W1,
                    L0_ffn_W2, L0_ffn_W3):
    QLR, KVLR, FFNH = cfg["QLR"], cfg["KVLR"], cfg["FFNH"]
    QC, KC = QLR // 128, KVLR // 128
    perm = w1_perm(FFNH)
    w = {}
    for name, val, tf in (
        ("wqaT", L0_attn_w_qa, lambda a: a.T),
        ("wkvaT", L0_attn_w_kva, lambda a: a.T),
        ("wqbT", L0_attn_w_qb, lambda a: a.T),
        ("wkvbT", L0_attn_w_kvb, lambda a: a.T),
        ("woT", L0_attn_w_o, lambda a: a.T),
        ("w1pt", L0_ffn_W1, lambda a: a.T[:, perm]),
        ("w3pt", L0_ffn_W3, lambda a: a.T[:, perm]),
        ("w2t", L0_ffn_W2, lambda a: a.T),
    ):
        hi, lo = _hilo(np.asarray(val))
        w[name] = np.ascontiguousarray(tf(hi))
        w[name + "_lo"] = np.ascontiguousarray(tf(lo))
    w["gq"] = np.ascontiguousarray(
        np.asarray(L0_attn_q_a_ln_w, np.float32).reshape(QC, 128).T)
    w["gkv"] = np.ascontiguousarray(
        np.asarray(L0_attn_kv_a_ln_w, np.float32).reshape(KC, 128).T)
    return w


def kernel(L0_attn_hidden_states, L0_attn_w_qa, L0_attn_q_a_ln_w, L0_attn_w_qb,
           L0_attn_w_kva, L0_attn_kv_a_ln_w, L0_attn_w_kvb, L0_attn_w_o,
           L0_ffn_W1, L0_ffn_W2, L0_ffn_W3):
    cfg = dict(REAL_CFG)
    cfg["LO"] = any(
        np.asarray(a).dtype == np.float64
        for a in (L0_attn_w_qa, L0_attn_w_qb, L0_attn_w_kva, L0_attn_w_kvb,
                  L0_attn_w_o, L0_ffn_W1, L0_ffn_W2, L0_ffn_W3))
    x = np.ascontiguousarray(np.asarray(L0_attn_hidden_states, dtype=np.float32))
    B = x.shape[0]
    w = _shared_weights(cfg, L0_attn_w_qa, L0_attn_q_a_ln_w, L0_attn_w_qb,
                        L0_attn_w_kva, L0_attn_kv_a_ln_w, L0_attn_w_kvb,
                        L0_attn_w_o, L0_ffn_W1, L0_ffn_W2, L0_ffn_W3)
    nc = build_nc(cfg)
    in_maps = []
    owns = []
    for c in range(8):
        b, r = c // 2, c % 2
        d, own = _prep_inputs(cfg, x, w, b, r)
        in_maps.append(d)
        owns.append((b, own))
    res = run_bass_kernel_spmd(nc, in_maps, core_ids=list(range(8)))
    outp = np.empty_like(x)
    for c in range(8):
        b, own = owns[c]
        outp[b, own, :] = np.asarray(res.results[c]["out"], np.float32).T
    return outp
